# version 04
# baseline (speedup 1.0000x reference)
import numpy as np
import ml_dtypes
from contextlib import ExitStack
import concourse.bass as bass
import concourse.mybir as mybir
from concourse.bass_utils import run_bass_kernel_spmd

F32 = mybir.dt.float32
BF16 = mybir.dt.bfloat16
AF = mybir.ActivationFunctionType
ALU = mybir.AluOpType
AX = mybir.AxisListType.X

B, S, D = 2, 4096, 2048
H, DH = 16, 128
HID1, HID2 = 5461, 8192
T = 1024
NT = T // 128
NK = D // 128
NPRE = 3
EPS = 1e-5
SCL = float(DH) ** -0.5

ENG = ("sync", "scalar", "vector", "tensor", "gpsimd")


def ts(i, n):
    return slice(i * n, (i + 1) * n)


DMA_LANES = 4


class P:
    def __init__(self, nc):
        self.nc = nc
        self.ops = {e: [] for e in ENG}
        self.sem = {}
        self.cnt = {}
        self.hw = {e: {} for e in ENG}
        self.dma_i = {"ss": 0, "sa": 0, "sq": 0}

    def add_sem(self, name, handle):
        self.sem[name] = handle
        self.cnt[name] = 0

    def wait(self, eng, dep):
        if dep is None:
            return
        s, v = dep
        if v <= 0:
            return
        if self.hw[eng].get(s, 0) >= v:
            return
        self.hw[eng][s] = v
        h = self.sem[s]
        self.ops[eng].append(lambda E, h=h, v=v: E.wait_ge(h, v))

    def waits(self, eng, deps):
        for d in deps:
            self.wait(eng, d)

    def fence_all(self):
        return [(s, self.cnt[s]) for s in self.sem]

    def dma(self, eng, out, in_, waits=(), transpose=False):
        base = {"sync": "ss", "scalar": "sa", "gpsimd": "sq"}[eng]
        i = self.dma_i[base]
        self.dma_i[base] = i + 1
        lane = f"{base}{i % DMA_LANES}"
        self.wait(eng, (lane, 16 * (i // DMA_LANES)))
        self.waits(eng, waits)
        v = self.cnt[lane] = 16 * (i // DMA_LANES + 1)
        h = self.sem[lane]

        def f(E, out=out, in_=in_, h=h, transpose=transpose):
            E.dma_start(out=out, in_=in_, transpose=transpose).then_inc(h, 16)

        self.ops[eng].append(f)
        return (lane, v)

    def op(self, eng, fn, waits=(), inc=True):
        sname = {"tensor": "pt", "vector": "pv", "scalar": "pc", "gpsimd": "pg"}[eng]
        if eng != "tensor":
            inc = True
            self.wait(eng, (sname, self.cnt[sname]))
        self.waits(eng, waits)
        if inc:
            v = self.cnt[sname] = self.cnt[sname] + 1
            h = self.sem[sname]

            def f(E, fn=fn, h=h):
                fn(E).then_inc(h, 1)

            self.ops[eng].append(f)
            return (sname, v)
        self.ops[eng].append(lambda E, fn=fn: fn(E))
        return None


def build_program():
    nc = bass.Bass("TRN2", target_bir_lowering=False)

    di = lambda n, sh, dt=F32: nc.dram_tensor(n, sh, dt, kind="ExternalInput").ap()
    x = di("x", [T, D])
    xpre = di("xpre", [NPRE * T, D])
    wq, wk, wv, wg, wo = (di(n, [D, D]) for n in ("wq", "wk", "wv", "wg", "wo"))
    w1, w2 = di("w1", [D, HID1]), di("w2", [D, HID1])
    w3 = di("w3", [HID1, D])
    wgate, wup = di("wgate", [HID2, D]), di("wup", [HID2, D])
    wdown = di("wdown", [D, HID2])
    dec = di("dec", [H, 8, 128, 512])
    gqrow = di("gqrow", [1, H * T], BF16)
    kgc = di("kgc", [128, NPRE * NT * H])
    n1w, n1b = di("n1w", [1, D]), di("n1b", [1, D])
    n2w, n2b = di("n2w", [1, D]), di("n2b", [1, D])
    out = nc.dram_tensor("out", [T, D], F32, kind="ExternalOutput").ap()

    dd = lambda n, sh, dt: nc.dram_tensor(n, sh, dt).ap()
    s1 = dd("s1", [T, D], BF16)
    x2d = dd("x2d", [T, D], F32)
    xBd = dd("xBd", [T, D], F32)
    wgate_bf = dd("wgate_bf", [HID2, D], BF16)
    wup_bf = dd("wup_bf", [HID2, D], BF16)
    wdown_bf = dd("wdown_bf", [D, HID2], BF16)
    wdownT = dd("wdownT", [HID2, D], BF16)

    with ExitStack() as st:
        e = st.enter_context
        p = P(nc)
        for base in ("ss", "sa", "sq"):
            for l in range(DMA_LANES):
                p.add_sem(f"{base}{l}", e(nc.semaphore(f"{base}{l}")))
        for sname in ("pt", "pv", "pc", "pg"):
            p.add_sem(sname, e(nc.semaphore(sname)))

        arena = e(nc.sbuf_tensor("arena", [128, NK * T], BF16, is_transpose_dest=True))
        stats = e(nc.sbuf_tensor("stats", [128, 64], F32))
        kgc_s = e(nc.sbuf_tensor("kgc_s", [128, NPRE * NT * H], F32))
        ones1 = e(nc.sbuf_tensor("ones1", [1, 128], BF16))
        onesS = e(nc.sbuf_tensor("onesS", [128, 128], F32))
        gq_s = e(nc.sbuf_tensor("gq_s", [1, T], BF16))
        epsb = e(nc.sbuf_tensor("epsb", [128, 1], F32))
        eps2b = e(nc.sbuf_tensor("eps2b", [128, 1], F32))
        pb = [e(nc.psum_tensor(f"pb{i}", [128, 512], F32)) for i in range(8)]
        bank_free = [None] * 8

        p.op("vector", lambda E: E.memset(ones1[:, :], 1.0), inc=False)
        p.op("vector", lambda E: E.memset(epsb[:, :], EPS), inc=False)
        p.op("vector", lambda E: E.memset(eps2b[:, :], EPS * EPS), inc=False)
        d_ones = p.op("vector", lambda E: E.memset(onesS[:, :], 1.0))
        d_kgc = p.dma("sync", kgc_s[:, :], kgc)

        d_gbf = [p.dma("gpsimd", wgate_bf[ts(i, 2048), :], wgate[ts(i, 2048), :])
                 for i in range(4)]
        d_ubf = [p.dma("gpsimd", wup_bf[ts(i, 2048), :], wup[ts(i, 2048), :])
                 for i in range(4)]
        d_dbf = [p.dma("gpsimd", wdown_bf[ts(i, 512), :], wdown[ts(i, 512), :])
                 for i in range(4)]

        s1_free = [[]]

        def wload(eng, dst_ap, src_ap, n, waits=()):
            return p.dma(eng, dst_ap.rearrange("p (a n) -> p a n", n=n),
                         src_ap.rearrange("(a p) n -> p a n", p=128), waits=waits)

        def replicate_rows(dst, dep):
            deps = []
            for j in range(4):
                dm = p.op("tensor", lambda E, j=j, dst=dst: E.matmul(
                    pb[7][:, 0:512], onesS[0:1, :], dst[0:1, ts(j, 512)],
                    start=True, stop=True), waits=[dep, d_ones, bank_free[7]])
                dc = p.op("vector", lambda E, j=j, dst=dst: E.tensor_copy(
                    dst[:, ts(j, 512)], pb[7][:, 0:512]), waits=[dm])
                bank_free[7] = dc
                deps.append(dc)
            return deps

        _uid = [0]

        def ln_block(src_ap, dst_arena, affine, col0):
            fence = p.fence_all()
            _uid[0] += 1
            u = _uid[0]
            with ExitStack() as ls:
                le = ls.enter_context
                xt = le(nc.sbuf_tensor(f"ln_xt{u}", [128, 2 * 2048], F32))
                sq = le(nc.sbuf_tensor(f"ln_sq{u}", [128, 2048], F32))
                tst = le(nc.sbuf_tensor(f"ln_tst{u}", [128, 16], F32))
                ht = le(nc.sbuf_tensor(f"ln_ht{u}", [128, 2 * 2048], BF16))
                nrm = sq
                if affine is not None:
                    wrep = le(nc.sbuf_tensor(f"ln_wrep{u}", [128, 2048], F32))
                    brep = le(nc.sbuf_tensor(f"ln_brep{u}", [128, 2048], F32))
                    dw_ = p.dma("sync", wrep[0:1, :], affine[1], waits=fence)
                    db_ = p.dma("sync", brep[0:1, :], affine[2], waits=fence)
                    rep = replicate_rows(wrep, dw_) + replicate_rows(brep, db_)
                else:
                    rep = []
                st_deps = []
                prev_cons = [None, None]
                prev_stage = [None, None]
                sq_free = None
                for t in range(NT):
                    sl = ts(t % 2, 2048)
                    dld = p.dma("sync", xt[:, sl], src_ap[ts(t, 128), :],
                                waits=[prev_cons[t % 2]] + rep + fence)
                    dsum = p.op("vector", lambda E, sl=sl: E.reduce_sum(
                        tst[:, 0:1], xt[:, sl], axis=AX), waits=[dld] + fence)
                    dsq = p.op("scalar", lambda E, sl=sl: E.activation(
                        sq[:, :], xt[:, sl], AF.Square), waits=[dld, sq_free] + fence)
                    dsum2 = p.op("vector", lambda E: E.reduce_sum(
                        tst[:, 1:2], sq[:, :], axis=AX), waits=[dsq])
                    sq_free = dsum2
                    cn, cr = col0 + 2 * t, col0 + 2 * t + 1
                    p.op("vector", lambda E, cn=cn: E.tensor_scalar_mul(
                        stats[:, cn:cn + 1], tst[:, 0:1], -1.0 / D),
                        waits=[dsum], inc=False)
                    p.op("vector", lambda E: E.tensor_scalar_mul(
                        tst[:, 2:3], tst[:, 1:2], 1.0 / D), waits=[dsum2], inc=False)
                    p.op("vector", lambda E, cn=cn: E.tensor_mul(
                        tst[:, 3:4], stats[:, cn:cn + 1], stats[:, cn:cn + 1]),
                        inc=False)
                    dvar = p.op("vector", lambda E: E.tensor_sub(
                        tst[:, 4:5], tst[:, 2:3], tst[:, 3:4]))
                    dln_ = p.op("scalar", lambda E: E.activation(
                        tst[:, 5:6], tst[:, 4:5], AF.Ln, bias=epsb[:, 0:1]), waits=[dvar])
                    drs = p.op("scalar", lambda E, cr=cr: E.activation(
                        stats[:, cr:cr + 1], tst[:, 5:6], AF.Exp, scale=-0.5),
                        waits=[dln_])
                    if affine is not None and affine[0] == "combined":
                        cc1 = col0 + 16 + t
                        dl2 = p.op("scalar", lambda E: E.activation(
                            tst[:, 9:10], tst[:, 4:5], AF.Ln,
                            bias=eps2b[:, 0:1], scale=1.0 + EPS), waits=[dvar])
                        dsc = p.op("scalar", lambda E, cc1=cc1: E.activation(
                            stats[:, cc1:cc1 + 1], tst[:, 9:10], AF.Exp,
                            scale=-0.5), waits=[dl2])
                        scol = cc1
                    else:
                        dsc, scol = drs, cr
                    hsl = ts(t % 2, 2048)
                    if affine is None:
                        dn = p.op("vector", lambda E, sl=sl, hsl=hsl, cn=cn, scol=scol:
                                  E.tensor_scalar(ht[:, hsl], xt[:, sl],
                                                  stats[:, cn:cn + 1],
                                                  stats[:, scol:scol + 1],
                                                  ALU.add, ALU.mult),
                                  waits=[dsc, prev_stage[t % 2]])
                    else:
                        d0 = p.op("vector", lambda E, sl=sl, cn=cn, scol=scol:
                                  E.tensor_scalar(nrm[:, :], xt[:, sl],
                                                  stats[:, cn:cn + 1],
                                                  stats[:, scol:scol + 1],
                                                  ALU.add, ALU.mult),
                                  waits=[dsc])
                        d1 = p.op("vector", lambda E: E.tensor_mul(
                            nrm[:, :], nrm[:, :], wrep[:, :]), waits=[d0])
                        dn = p.op("vector", lambda E, hsl=hsl: E.tensor_add(
                            ht[:, hsl], nrm[:, :], brep[:, :]),
                            waits=[d1, prev_stage[t % 2]])
                    prev_cons[t % 2] = dn
                    sq_free = dn
                    dst_ = p.dma("sync", s1[ts(t, 128), :], ht[:, hsl],
                                 waits=[dn] + s1_free[0])
                    prev_stage[t % 2] = dst_
                    st_deps.append(dst_)
                tr_deps = []
                pe_now = ("pt", p.cnt["pt"])
                for k in range(NK):
                    dtr = p.dma("scalar", dst_arena[:, ts(k, T)], s1[:, ts(k, 128)],
                                waits=st_deps + [pe_now], transpose=True)
                    tr_deps.append(dtr)
                s1_free[0] = tr_deps
                return tr_deps

        with ExitStack() as hkv_st:
            he = hkv_st.enter_context
            h_f = he(nc.sbuf_tensor("h_f", [128, NK * T], BF16, is_transpose_dest=True))
            r_bf = he(nc.sbuf_tensor("r_bf", [128, H * 128], BF16))

            tr_h = ln_block(x, h_f, ("combined", n1w, n1b), 0)

            with ExitStack() as pre_st:
                r_acc = pre_st.enter_context(
                    nc.sbuf_tensor("r_acc", [128, H * 128], F32))
                d_racc0 = p.op("vector", lambda E: E.memset(r_acc[:, :], 0.0))
                for c in range(NPRE):
                    tr_hp = ln_block(xpre[ts(c, T), :], arena,
                                     ("combined", n1w, n1b), 32)
                    fence = p.fence_all()
                    with ExitStack() as ch_st:
                        ce = ch_st.enter_context
                        kw_sb = ce(nc.sbuf_tensor(f"kw{c}", [128, NT * D], BF16))
                        vw_sb = ce(nc.sbuf_tensor(f"vw{c}", [128, NT * D], BF16))
                        wslab = ce(nc.sbuf_tensor(f"wslab{c}", [128, NK * 512], BF16))
                        w_dep = None
                        ev_last = None
                        for s4 in range(4):
                            for kv in range(2):
                                wsrc = wk if kv == 0 else wv
                                w_dep = wload("gpsimd", wslab[:, :],
                                              wsrc[:, ts(s4, 512)], 512,
                                              waits=[w_dep] + fence)
                                for t in range(NT):
                                    bk = pb[t % 2]
                                    p.waits("tensor",
                                            [w_dep, bank_free[t % 2]] + tr_hp)
                                    for k in range(NK):
                                        def f(E, k=k, t=t, bk=bk):
                                            return E.matmul(
                                                bk[:, 0:512],
                                                arena[:, k * T + t * 128: k * T + t * 128 + 128],
                                                wslab[:, k * 512:(k + 1) * 512],
                                                start=(k == 0), stop=(k == NK - 1))
                                        dmm = p.op("tensor", f, inc=(k == NK - 1))
                                    if kv == 0:
                                        evs = []
                                        for hh in range(4):
                                            head = 4 * s4 + hh
                                            kc = (c * NT + t) * H + head
                                            ev = p.op(
                                                "vector",
                                                lambda E, t=t, hh=hh, head=head, kc=kc, bk=bk:
                                                E.tensor_scalar_mul(
                                                    kw_sb[:, t * D + head * 128: t * D + (head + 1) * 128],
                                                    bk[:, hh * 128:(hh + 1) * 128],
                                                    kgc_s[:, kc:kc + 1]),
                                                waits=[dmm, d_kgc] + fence)
                                            evs.append(ev)
                                        bank_free[t % 2] = evs[-1]
                                        ev_last = evs[-1]
                                    else:
                                        ev = p.op(
                                            "vector",
                                            lambda E, t=t, s4=s4, bk=bk:
                                            E.tensor_copy(
                                                vw_sb[:, t * D + s4 * 512: t * D + (s4 + 1) * 512],
                                                bk[:, 0:512]),
                                            waits=[dmm] + fence)
                                        bank_free[t % 2] = ev
                                        ev_last = ev
                                w_dep = dmm
                        for q in range(4):
                            bq = pb[4 + q]
                            p.waits("tensor", [ev_last, bank_free[4 + q]])
                            for hh in range(4):
                                head = 4 * q + hh
                                for t in range(NT):
                                    def f(E, t=t, hh=hh, head=head, bq=bq):
                                        return E.matmul(
                                            bq[:, hh * 128:(hh + 1) * 128],
                                            kw_sb[:, t * D + head * 128: t * D + (head + 1) * 128],
                                            vw_sb[:, t * D + head * 128: t * D + (head + 1) * 128],
                                            start=(t == 0), stop=(t == NT - 1))
                                    dmr = p.op("tensor", f, inc=(t == NT - 1))
                            dadd = p.op("vector", lambda E, q=q, bq=bq: E.tensor_add(
                                r_acc[:, ts(q, 512)], r_acc[:, ts(q, 512)],
                                bq[:, 0:512]), waits=[dmr, d_racc0])
                            bank_free[4 + q] = dadd
                d_rbf = p.op("vector", lambda E: E.tensor_copy(
                    r_bf[:, :], r_acc[:, :]))

            k_f = he(nc.sbuf_tensor("k_f", [128, H * T], BF16))
            v_T = he(nc.sbuf_tensor("v_T", [128, NT * D], BF16))
            wcb = he(nc.sbuf_tensor("wcb", [128, 2 * 2048], BF16))
            wcb_free = [None, None]
            m_st = ExitStack()
            wslab = m_st.enter_context(
                nc.sbuf_tensor("wslab_m", [128, 2 * 8192], BF16))
            fence = p.fence_all()
            wv_dep = [None, None]
            for s in range(4):
                buf = s % 2
                vo = buf * 8192
                wv_dep[buf] = wload(
                    "gpsimd", wslab[:, vo:vo + 8192],
                    wv[:, ts(s, 512)], 512,
                    waits=[wv_dep[buf]] + fence)
                ev = None
                for t in range(NT):
                    bv = pb[t % 4]
                    p.waits("tensor", [wv_dep[buf], bank_free[t % 4]] + tr_h)
                    for k in range(NK):
                        def f(E, k=k, t=t, bv=bv, vo=vo):
                            return E.matmul(
                                bv[:, 0:512],
                                h_f[:, k * T + t * 128: k * T + t * 128 + 128],
                                wslab[:, vo + k * 512: vo + (k + 1) * 512],
                                start=(k == 0), stop=(k == NK - 1))
                        dmv = p.op("tensor", f, inc=(k == NK - 1))
                    ev = p.op("vector", lambda E, s=s, t=t, bv=bv: E.tensor_copy(
                        v_T[:, t * 2048 + s * 512: t * 2048 + (s + 1) * 512],
                        bv[:, 0:512]), waits=[dmv])
                    bank_free[t % 4] = ev
                wv_dep[buf] = ev

            fence_m = p.fence_all()

            def feat_proj(wdram, head, evac_cb, banks=(0, 1)):
                buf = head % 2
                off = buf * 2048
                wdep = wload("gpsimd", wcb[:, off:off + 2048],
                             wdram[:, ts(head, 128)], 128,
                             waits=[wcb_free[buf]] + fence_m)
                outs = []
                last_mm = None
                for half in range(2):
                    bk = pb[banks[half]]
                    p.waits("tensor", [wdep, bank_free[banks[half]]] + tr_h)
                    for k in range(NK):
                        def f(E, k=k, half=half, bk=bk, off=off):
                            return E.matmul(
                                bk[:, 0:512],
                                wcb[:, off + k * 128: off + (k + 1) * 128],
                                h_f[:, k * T + half * 512: k * T + (half + 1) * 512],
                                start=(k == 0), stop=(k == NK - 1))
                        dmm = p.op("tensor", f, inc=(k == NK - 1))
                    last_mm = dmm
                    ev = evac_cb(half, bk, dmm)
                    bank_free[banks[half]] = ev
                    outs.append(ev)
                wcb_free[buf] = last_mm
                return outs

            for head in range(H):
                def evk(half, bk, gdep, head=head):
                    return p.op("vector", lambda E, half=half, bk=bk, head=head:
                                E.tensor_scalar_mul(
                                    k_f[:, head * T + half * 512: head * T + (half + 1) * 512],
                                    bk[:, 0:512], SCL), waits=[gdep])
                feat_proj(wk, head, evk)
            m_st.close()

            with ExitStack() as rs:
                re_ = rs.enter_context
                qb = re_(nc.sbuf_tensor("qb", [128, T], BF16))
                qg = re_(nc.sbuf_tensor("qg", [128, T], BF16))
                sbf = re_(nc.sbuf_tensor("sbf", [128, 12 * 512], BF16))
                decb = re_(nc.sbuf_tensor("decb", [128, 3 * 512], F32))
                retsb = re_(nc.sbuf_tensor("retsb", [128, T], F32))
                gnt = re_(nc.sbuf_tensor("gnt", [128, T], F32))
                negmb = re_(nc.sbuf_tensor("negmb", [128, T], F32))
                rstdb = re_(nc.sbuf_tensor("rstdb", [128, T], F32))
                gsb = re_(nc.sbuf_tensor("gsb", [128, T], F32))
                fence = p.fence_all()

                pairs = [(0, 0), (0, 1), (0, 2), (0, 3),
                         (1, 0), (1, 1), (1, 2), (1, 3),
                         (1, 4), (1, 5), (1, 6), (1, 7)]
                dec_slot = [None, None, None]
                q_cons = None
                g_cons = None
                gq_cons = None
                head_done = None
                sbf_free = None
                for head in range(H):
                    def evq(half, bk, gdep):
                        return p.op("vector", lambda E, half=half, bk=bk:
                                    E.tensor_copy(qb[:, ts(half, 512)],
                                                  bk[:, 0:512]),
                                    waits=[gdep, q_cons] + fence)
                    qouts = feat_proj(wq, head, evq)

                    dgq = p.dma("sync", gq_s[0:1, :], gqrow[0:1, ts(head, T)],
                                waits=[gq_cons])
                    qg_deps = []
                    for half in range(2):
                        p.waits("tensor", [dgq, qouts[half], bank_free[6]])
                        dbc = p.op("tensor", lambda E, half=half: E.matmul(
                            pb[6][:, 0:512], ones1[0:1, :],
                            gq_s[0:1, ts(half, 512)], start=True, stop=True))
                        dqg = p.op("vector", lambda E, half=half: E.tensor_mul(
                            qg[:, ts(half, 512)], qb[:, ts(half, 512)],
                            pb[6][:, 0:512]), waits=[dbc])
                        bank_free[6] = dqg
                        qg_deps.append(dqg)
                    gq_cons = qg_deps[-1]

                    sc_deps = {}
                    for i, (qt, kt) in enumerate(pairs):
                        r = 4 * qt - kt + 3
                        ddec = p.dma(
                            "scalar", decb[:, ts(i % 3, 512)], dec[head, r, :, :],
                            waits=[dec_slot[i % 3]] + fence)
                        bk = pb[2 + (i % 2)]
                        p.waits("tensor", [qouts[qt], bank_free[2 + (i % 2)]])
                        dsc = p.op("tensor", lambda E, qt=qt, kt=kt, bk=bk, head=head:
                                   E.matmul(
                                       bk[:, 0:512],
                                       k_f[:, head * T + kt * 128: head * T + kt * 128 + 128],
                                       qb[:, ts(qt, 512)], start=True, stop=True))
                        slot = kt if qt == 0 else 4 + kt
                        dsb = p.op("vector", lambda E, slot=slot, i=i, bk=bk:
                                   E.tensor_mul(sbf[:, ts(slot, 512)], bk[:, 0:512],
                                                decb[:, ts(i % 3, 512)]),
                                   waits=[dsc, ddec, sbf_free])
                        dec_slot[i % 3] = dsb
                        bank_free[2 + (i % 2)] = dsb
                        sc_deps[(qt, kt)] = dsb

                    ret_deps = []
                    last_ret_mm = None
                    for qt in range(2):
                        bk = pb[4 + qt]
                        nkt = 4 if qt == 0 else 8
                        p.waits("tensor", [qg_deps[qt], d_rbf, bank_free[4 + qt]])
                        p.op("tensor", lambda E, qt=qt, bk=bk, head=head: E.matmul(
                            bk[:, 0:512], r_bf[:, ts(head, 128)],
                            qg[:, ts(qt, 512)], start=True, stop=False), inc=False)
                        for kt in range(nkt):
                            slot = kt if qt == 0 else 4 + kt
                            p.waits("tensor", [sc_deps[(qt, kt)]])

                            def f(E, qt=qt, kt=kt, slot=slot, bk=bk, head=head):
                                return E.matmul(
                                    bk[:, 0:512],
                                    v_T[:, kt * 2048 + head * 128: kt * 2048 + (head + 1) * 128],
                                    sbf[:, ts(slot, 512)],
                                    start=False, stop=(kt == nkt - 1))
                            dmm = p.op("tensor", f, inc=(kt == nkt - 1))
                        last_ret_mm = dmm
                        dev = p.op("vector", lambda E, qt=qt, bk=bk: E.tensor_copy(
                            retsb[:, ts(qt, 512)], bk[:, 0:512]),
                            waits=[dmm, head_done])
                        bank_free[4 + qt] = dev
                        ret_deps.append(dev)
                    sbf_free = last_ret_mm

                    dsq = p.op("scalar", lambda E: E.activation(
                        gnt[:, :], retsb[:, :], AF.Square),
                        waits=ret_deps + [head_done] + fence)
                    stat_deps = []
                    for half in range(2):
                        p.waits("tensor", ret_deps + [bank_free[6]])
                        dm = p.op("tensor", lambda E, half=half: E.matmul(
                            pb[6][:, 0:512], onesS[:, :], retsb[:, ts(half, 512)],
                            start=True, stop=True))
                        p.waits("tensor", [dsq, bank_free[7]])
                        ds2 = p.op("tensor", lambda E, half=half: E.matmul(
                            pb[7][:, 0:512], onesS[:, :], gnt[:, ts(half, 512)],
                            start=True, stop=True))
                        dnm = p.op("vector", lambda E, half=half:
                                   E.tensor_scalar_mul(negmb[:, ts(half, 512)],
                                                       pb[6][:, 0:512], -1.0 / 128),
                                   waits=[dm])
                        bank_free[6] = dnm
                        dm2 = p.op("vector", lambda E, half=half: E.tensor_mul(
                            gnt[:, ts(half, 512)], negmb[:, ts(half, 512)],
                            negmb[:, ts(half, 512)]), waits=[dnm, dsq])
                        dvr = p.op("vector", lambda E, half=half:
                                   E.scalar_tensor_tensor(
                                       gnt[:, ts(half, 512)], pb[7][:, 0:512],
                                       1.0 / 128, gnt[:, ts(half, 512)],
                                       ALU.mult, ALU.subtract),
                                   waits=[ds2, dm2])
                        bank_free[7] = dvr
                        dl = p.op("scalar", lambda E, half=half: E.activation(
                            rstdb[:, ts(half, 512)], gnt[:, ts(half, 512)],
                            AF.Ln, bias=epsb[:, 0:1]), waits=[dvr])
                        dr = p.op("scalar", lambda E, half=half: E.activation(
                            rstdb[:, ts(half, 512)], rstdb[:, ts(half, 512)],
                            AF.Exp, scale=-0.5), waits=[dl])
                        stat_deps.append(dr)

                    def evg(half, bk, gdep):
                        return p.op("scalar", lambda E, half=half, bk=bk:
                                    E.activation(gsb[:, ts(half, 512)],
                                                 bk[:, 0:512], AF.Silu),
                                    waits=[gdep, g_cons] + fence)
                    gouts = feat_proj(wg, head, evg, banks=(2, 3))
                    g_cons = gouts[-1]

                    a_deps = []
                    for half in range(2):
                        d1 = p.op("vector", lambda E, half=half: E.tensor_add(
                            retsb[:, ts(half, 512)], retsb[:, ts(half, 512)],
                            negmb[:, ts(half, 512)]), waits=stat_deps + ret_deps)
                        d2 = p.op("vector", lambda E, half=half: E.tensor_mul(
                            retsb[:, ts(half, 512)], retsb[:, ts(half, 512)],
                            rstdb[:, ts(half, 512)]), waits=[d1])
                        d3 = p.op("vector", lambda E, half=half, head=head:
                                  E.tensor_mul(
                                      arena[:, head * T + half * 512: head * T + (half + 1) * 512],
                                      retsb[:, ts(half, 512)],
                                      gsb[:, ts(half, 512)]),
                                  waits=[d2, gouts[half]])
                        a_deps.append(d3)
                    q_cons = a_deps[-1]
                    head_done = a_deps[-1]
                a_done = head_done

        with ExitStack() as ys:
            ye = ys.enter_context
            woq = ye(nc.sbuf_tensor("woq", [128, 2 * 8192], BF16))
            yx = ye(nc.sbuf_tensor("yx", [128, 2 * 512], F32))
            yst = ye(nc.sbuf_tensor("yst", [128, 2 * 512], F32))
            tstg = ye(nc.sbuf_tensor("tstg", [128, 4 * 2048], BF16,
                                     is_transpose_dest=True))
            fence = p.fence_all()

            wo_dep = [None, None]
            yx_dep = [None, None]
            yst_dep = [None, None]
            x2_last = None
            last_y_mm = [None, None]
            for s in range(4):
                buf = s % 2
                wo_dep[buf] = wload(
                    "gpsimd", woq[:, ts(buf, 8192)],
                    wo[:, ts(s, 512)], 512,
                    waits=[wo_dep[buf], last_y_mm[buf]] + fence)
                for t in range(NT):
                    bk = pb[t % 2]
                    p.waits("tensor", [wo_dep[buf], a_done, bank_free[t % 2]])
                    for k in range(NK):
                        def f(E, k=k, t=t, bk=bk, buf=buf):
                            return E.matmul(
                                bk[:, 0:512],
                                arena[:, k * T + t * 128: k * T + t * 128 + 128],
                                woq[:, buf * 8192 + k * 512: buf * 8192 + (k + 1) * 512],
                                start=(k == 0), stop=(k == NK - 1))
                        dmm = p.op("tensor", f, inc=(k == NK - 1))
                    last_y_mm[buf] = dmm
                    xb = t % 2
                    dx = p.dma("sync", yx[:, ts(xb, 512)], x[ts(t, 128), ts(s, 512)],
                               waits=[yx_dep[xb]] + fence)
                    cn = 2 * t
                    d0 = p.op("vector", lambda E, xb=xb, cn=cn: E.tensor_scalar(
                        yst[:, ts(xb, 512)], yx[:, ts(xb, 512)],
                        stats[:, cn:cn + 1], stats[:, cn + 1:cn + 2],
                        ALU.add, ALU.mult),
                        waits=[dx, yst_dep[xb]] + fence)
                    yx_dep[xb] = d0
                    d1 = p.op("vector", lambda E, xb=xb, bk=bk: E.tensor_add(
                        yst[:, ts(xb, 512)], yst[:, ts(xb, 512)], bk[:, 0:512]),
                        waits=[d0, dmm])
                    bank_free[t % 2] = d1
                    dst_ = p.dma("sync", x2d[ts(t, 128), ts(s, 512)],
                                 yst[:, ts(xb, 512)], waits=[d1])
                    yst_dep[xb] = dst_
                    x2_last = dst_

            tstg_dep = [None, None]
            dT_last = None
            for i in range(HID2 // 128):
                slot = i % 2
                dtr = p.dma("scalar", tstg[:, slot * 2048: slot * 2048 + 2048],
                            wdown_bf[0:D, ts(i, 128)],
                            waits=[tstg_dep[slot]] + d_dbf + fence, transpose=True)
                dst_ = p.dma("sync", wdownT[ts(i, 128), 0:D],
                             tstg[:, slot * 2048: slot * 2048 + 2048], waits=[dtr])
                tstg_dep[slot] = dst_
                dT_last = dst_

        tr_h2 = ln_block(x2d, arena, ("plain-affine", n2w, n2b), 32)

        with ExitStack() as f1s:
            fe = f1s.enter_context
            a2 = fe(nc.sbuf_tensor("a2", [128, 43 * 1024], BF16))
            w12cb = fe(nc.sbuf_tensor("w12cb", [128, 4 * 2048], BF16))
            silu_t = fe(nc.sbuf_tensor("silu_t", [128, 2 * 512], F32))
            fence = p.fence_all()

            w12_dep = [None, None, None, None]
            silu_dep = [None, None]
            a2_last = None
            for ch in range(43):
                ncc = 128 if ch < 42 else 85
                buf = ch % 2
                dw1 = wload("gpsimd", w12cb[:, buf * 2048: buf * 2048 + NK * ncc],
                            w1[:, ch * 128: ch * 128 + ncc], ncc,
                            waits=[w12_dep[buf]] + fence)
                dw2 = wload("gpsimd",
                            w12cb[:, (2 + buf) * 2048: (2 + buf) * 2048 + NK * ncc],
                            w2[:, ch * 128: ch * 128 + ncc], ncc,
                            waits=[w12_dep[2 + buf]] + fence)
                ch_dep = None
                for half in range(2):
                    b1 = pb[half]
                    b2_ = pb[2 + half]
                    p.waits("tensor", [dw1, bank_free[half]] + tr_h2)
                    for k in range(NK):
                        def f(E, k=k, half=half, b1=b1, buf=buf, ncc=ncc):
                            return E.matmul(
                                b1[0:ncc, 0:512],
                                w12cb[:, buf * 2048 + k * ncc: buf * 2048 + (k + 1) * ncc],
                                arena[:, k * T + half * 512: k * T + (half + 1) * 512],
                                start=(k == 0), stop=(k == NK - 1))
                        du1 = p.op("tensor", f, inc=(k == NK - 1))
                    p.waits("tensor", [dw2, bank_free[2 + half]])
                    for k in range(NK):
                        def f(E, k=k, half=half, b2_=b2_, buf=buf, ncc=ncc):
                            return E.matmul(
                                b2_[0:ncc, 0:512],
                                w12cb[:, (2 + buf) * 2048 + k * ncc: (2 + buf) * 2048 + (k + 1) * ncc],
                                arena[:, k * T + half * 512: k * T + (half + 1) * 512],
                                start=(k == 0), stop=(k == NK - 1))
                        du2 = p.op("tensor", f, inc=(k == NK - 1))
                    dsl = p.op("scalar", lambda E, half=half, b1=b1, ncc=ncc:
                               E.activation(silu_t[0:ncc, ts(half, 512)],
                                            b1[0:ncc, 0:512], AF.Silu),
                               waits=[du1, silu_dep[half]] + fence)
                    bank_free[half] = dsl
                    da2 = p.op("vector",
                               lambda E, ch=ch, half=half, b2_=b2_, ncc=ncc:
                               E.tensor_mul(
                                   a2[0:ncc, ch * 1024 + half * 512: ch * 1024 + (half + 1) * 512],
                                   silu_t[0:ncc, ts(half, 512)],
                                   b2_[0:ncc, 0:512]),
                               waits=[dsl, du2] + fence)
                    bank_free[2 + half] = da2
                    silu_dep[half] = da2
                    ch_dep = da2
                w12_dep[buf] = ch_dep
                w12_dep[2 + buf] = ch_dep
                a2_last = ch_dep

            with ExitStack() as f1b:
                fb = f1b.enter_context
                w3b = fb(nc.sbuf_tensor("w3b", [128, 2 * 11008], BF16))
                fxt = fb(nc.sbuf_tensor("fxt", [128, 4 * 256], F32))
                fst = fb(nc.sbuf_tensor("fst", [128, 2 * 256], F32))
                fence = p.fence_all()
                w3_dep = [None, None]
                fx_dep = [None, None, None, None]
                fst_dep = [None, None]
                xB_last = None
                last_f_mm = [None, None]
                for s8 in range(8):
                    buf = s8 % 2
                    off = buf * 11008
                    dw3a = wload("gpsimd", w3b[:, off: off + 42 * 256],
                                 w3[0:5376, ts(s8, 256)], 256,
                                 waits=[w3_dep[buf], last_f_mm[buf]] + fence)
                    dw3b = p.dma("gpsimd", w3b[0:85, off + 42 * 256: off + 43 * 256],
                                 w3[5376:5461, ts(s8, 256)])
                    for t in range(NT):
                        bk = pb[4 + t % 2]
                        p.waits("tensor", [dw3a, dw3b, bank_free[4 + t % 2], a2_last])
                        for ch in range(43):
                            ncc = 128 if ch < 42 else 85

                            def f(E, ch=ch, t=t, bk=bk, off=off, ncc=ncc):
                                return E.matmul(
                                    bk[:, 0:256],
                                    a2[0:ncc, ch * 1024 + t * 128: ch * 1024 + t * 128 + 128],
                                    w3b[0:ncc, off + ch * 256: off + (ch + 1) * 256],
                                    start=(ch == 0), stop=(ch == 42))
                            dmm = p.op("tensor", f, inc=(ch == 42))
                        last_f_mm[buf] = dmm
                        xb = t % 2
                        dx2 = p.dma("sync", fxt[:, ts(xb, 256)],
                                    x2d[ts(t, 128), ts(s8, 256)],
                                    waits=[fx_dep[xb]] + fence)
                        dxr = p.dma("sync", fxt[:, ts(2 + xb, 256)],
                                    x[ts(t, 128), ts(s8, 256)],
                                    waits=[fx_dep[2 + xb]] + fence)
                        d0 = p.op("vector", lambda E, xb=xb, bk=bk: E.tensor_add(
                            fst[:, ts(xb, 256)], fxt[:, ts(xb, 256)], bk[:, 0:256]),
                            waits=[dx2, dmm, fst_dep[xb]] + fence)
                        bank_free[4 + t % 2] = d0
                        fx_dep[xb] = d0
                        d1 = p.op("vector", lambda E, xb=xb: E.tensor_add(
                            fst[:, ts(xb, 256)], fst[:, ts(xb, 256)],
                            fxt[:, ts(2 + xb, 256)]), waits=[dxr, d0])
                        fx_dep[2 + xb] = d1
                        dst_ = p.dma("sync", xBd[ts(t, 128), ts(s8, 256)],
                                     fst[:, ts(xb, 256)], waits=[d1])
                        fst_dep[xb] = dst_
                        xB_last = dst_
                    w3_dep[buf] = xB_last

        tr_xb = ln_block(xBd, arena, None, 32)

        with ExitStack() as f2s:
            fe = f2s.enter_context
            a3 = fe(nc.sbuf_tensor("a3", [128, 64 * 512], BF16))
            wwin = fe(nc.sbuf_tensor("wwin", [128, 2 * 16384], BF16,
                                     is_transpose_dest=True))
            f2x = fe(nc.sbuf_tensor("f2x", [128, 2 * 512], F32))
            f2st = fe(nc.sbuf_tensor("f2st", [128, 2 * 512], F32))
            silu2 = fe(nc.sbuf_tensor("silu2", [128, 512], F32))
            fence = p.fence_all()

            win_dep = [None, None]
            f2x_dep = [None, None]
            f2st_dep = [None, None]
            a3_busy = [None] * 64
            wwin_free = None
            silu2_dep = None
            out_last = None
            for half in range(2):
                for w in range(16):
                    buf = w % 2
                    offg = buf * 16384
                    offu = buf * 16384 + 8192
                    gparts = []
                    uparts = []
                    for k in range(NK):
                        dgp = p.dma("scalar",
                                    wwin[:, offg + k * 512: offg + (k + 1) * 512],
                                    wgate_bf[ts(w, 512), ts(k, 128)],
                                    waits=[win_dep[buf], wwin_free] + d_gbf + fence,
                                    transpose=True)
                        gparts.append(dgp)
                        dup = p.dma("scalar",
                                    wwin[:, offu + k * 512: offu + (k + 1) * 512],
                                    wup_bf[ts(w, 512), ts(k, 128)],
                                    waits=d_ubf, transpose=True)
                        uparts.append(dup)
                    last = None
                    for hh in range(4):
                        hid = w * 4 + hh
                        bg = pb[hh % 2]
                        bu = pb[2 + hh % 2]
                        p.waits("tensor",
                                gparts + [bank_free[hh % 2], a3_busy[hid]] + tr_xb)
                        for k in range(NK):
                            def f(E, k=k, half=half, bg=bg, offg=offg, hh=hh):
                                return E.matmul(
                                    bg[:, 0:512],
                                    wwin[:, offg + k * 512 + hh * 128: offg + k * 512 + (hh + 1) * 128],
                                    arena[:, k * T + half * 512: k * T + (half + 1) * 512],
                                    start=(k == 0), stop=(k == NK - 1))
                            dg_ = p.op("tensor", f, inc=(k == NK - 1))
                        p.waits("tensor", uparts + [bank_free[2 + hh % 2]])
                        for k in range(NK):
                            def f(E, k=k, half=half, bu=bu, offu=offu, hh=hh):
                                return E.matmul(
                                    bu[:, 0:512],
                                    wwin[:, offu + k * 512 + hh * 128: offu + k * 512 + (hh + 1) * 128],
                                    arena[:, k * T + half * 512: k * T + (half + 1) * 512],
                                    start=(k == 0), stop=(k == NK - 1))
                            du_ = p.op("tensor", f, inc=(k == NK - 1))
                        wwin_last_mm = du_
                        dsl = p.op("scalar", lambda E, bg=bg: E.activation(
                            silu2[:, 0:512], bg[:, 0:512], AF.Silu),
                            waits=[dg_, silu2_dep] + fence)
                        bank_free[hh % 2] = dsl
                        da3 = p.op("vector", lambda E, hid=hid, bu=bu: E.tensor_mul(
                            a3[:, ts(hid, 512)], silu2[:, 0:512], bu[:, 0:512]),
                            waits=[dsl, du_] + fence)
                        silu2_dep = da3
                        bank_free[2 + hh % 2] = da3
                        a3_busy[hid] = da3
                        last = da3
                    win_dep[buf] = last
                    wwin_free = wwin_last_mm
                for s in range(4):
                    dwd0 = wload("gpsimd", wwin[:, 0:16384],
                                 wdownT[0:4096, ts(s, 512)], 512,
                                 waits=[win_dep[0], win_dep[1], dT_last, wwin_free])
                    dwd1 = wload("gpsimd", wwin[:, 16384:32768],
                                 wdownT[4096:8192, ts(s, 512)], 512)
                    win_dep[0] = None
                    win_dep[1] = None
                    for tt in range(4):
                        t = half * 4 + tt
                        bk = pb[4 + tt % 2]
                        p.waits("tensor",
                                [dwd0, dwd1, bank_free[4 + tt % 2], a3_busy[63]])
                        for hid in range(64):
                            def f(E, hid=hid, tt=tt, bk=bk):
                                return E.matmul(
                                    bk[:, 0:512],
                                    a3[:, hid * 512 + tt * 128: hid * 512 + (tt + 1) * 128],
                                    wwin[:, hid * 512: (hid + 1) * 512],
                                    start=(hid == 0), stop=(hid == 63))
                            dmm = p.op("tensor", f, inc=(hid == 63))
                        wwin_free = dmm
                        xb = tt % 2
                        dxB = p.dma("sync", f2x[:, ts(xb, 512)],
                                    xBd[ts(t, 128), ts(s, 512)],
                                    waits=[f2x_dep[xb]] + fence)
                        d0 = p.op("vector", lambda E, xb=xb, bk=bk: E.tensor_add(
                            f2st[:, ts(xb, 512)], f2x[:, ts(xb, 512)],
                            bk[:, 0:512]),
                            waits=[dxB, dmm, f2st_dep[xb]] + fence)
                        bank_free[4 + tt % 2] = d0
                        f2x_dep[xb] = d0
                        dst_ = p.dma("sync", out[ts(t, 128), ts(s, 512)],
                                     f2st[:, ts(xb, 512)], waits=[d0])
                        f2st_dep[xb] = dst_
                        out_last = dst_
                for i in range(64):
                    a3_busy[i] = out_last

            p.waits("sync", [out_last] +
                    [(f"ss{l}", p.cnt[f"ss{l}"]) for l in range(DMA_LANES)])

        with nc.Block() as blk:
            @blk.sync
            def _(E):
                for f in p.ops["sync"]:
                    f(E)

            @blk.scalar
            def _(E):
                for f in p.ops["scalar"]:
                    f(E)

            @blk.vector
            def _(E):
                for f in p.ops["vector"]:
                    f(E)

            @blk.tensor
            def _(E):
                for f in p.ops["tensor"]:
                    f(E)

            @blk.gpsimd
            def _(E):
                for f in p.ops["gpsimd"]:
                    f(E)

    return nc


def _host_tables():
    gam = 1.0 - 2.0 ** (-5.0 - np.arange(H, dtype=np.float64))
    f_ = np.arange(512)
    p_ = np.arange(128)
    decs = np.zeros((H, 8, 128, 512), np.float32)
    for h in range(H):
        for r in range(8):
            d_ = (r - 3) * 128 + f_[None, :] - p_[:, None]
            decs[h, r] = np.where(d_ >= 0, gam[h] ** np.maximum(d_, 0), 0.0
                                  ).astype(np.float32)
    a_ = np.arange(T, dtype=np.float64)
    gq = np.concatenate([(g ** (a_ + 1.0)) for g in gam]).astype(ml_dtypes.bfloat16)
    gq = np.ascontiguousarray(gq.reshape(1, H * T))
    return decs, gq, gam


def _kgc_for(loc, gam):
    kgc = np.zeros((128, NPRE * NT * H), np.float32)
    p_ = np.arange(128, dtype=np.float64)
    for c in range(NPRE):
        if c >= loc:
            continue
        for t in range(NT):
            pos = t * 128 + p_
            for h in range(H):
                expo = (1023.0 - pos) + 1024.0 * (loc - 1 - c)
                kgc[:, (c * NT + t) * H + h] = (gam[h] ** expo) * SCL
    return kgc


def kernel(L0_attn_x, L0_attn_norm1_w, L0_attn_norm1_b, L0_attn_wq, L0_attn_wk,
           L0_attn_wv, L0_attn_wg, L0_attn_wo, L0_attn_norm2_w, L0_attn_norm2_b,
           L0_attn_w1, L0_attn_w2, L0_attn_w3, L0_ffn_w_gate, L0_ffn_w_up,
           L0_ffn_w_down):
    xfull = np.ascontiguousarray(np.asarray(L0_attn_x, np.float32))
    decs, gq, gam = _host_tables()
    f32 = lambda a: np.ascontiguousarray(np.asarray(a, np.float32))
    shared = dict(
        wq=f32(L0_attn_wq), wk=f32(L0_attn_wk), wv=f32(L0_attn_wv),
        wg=f32(L0_attn_wg), wo=f32(L0_attn_wo),
        w1=f32(L0_attn_w1), w2=f32(L0_attn_w2), w3=f32(L0_attn_w3),
        wgate=f32(L0_ffn_w_gate), wup=f32(L0_ffn_w_up), wdown=f32(L0_ffn_w_down),
        dec=decs, gqrow=gq,
        n1w=f32(L0_attn_norm1_w).reshape(1, D),
        n1b=f32(L0_attn_norm1_b).reshape(1, D),
        n2w=f32(L0_attn_norm2_w).reshape(1, D),
        n2b=f32(L0_attn_norm2_b).reshape(1, D),
    )
    in_maps = []
    for c in range(8):
        seq, loc = c // 4, c % 4
        m = dict(shared)
        m["x"] = np.ascontiguousarray(xfull[seq, loc * T:(loc + 1) * T, :])
        m["xpre"] = np.ascontiguousarray(xfull[seq, 0:NPRE * T, :])
        m["kgc"] = _kgc_for(loc, gam)
        in_maps.append(m)

    nc = build_program()
    res = run_bass_kernel_spmd(nc, in_maps, core_ids=list(range(8)))
    outs = [np.asarray(res.results[c]["out"], np.float32) for c in range(8)]
    return np.stack([np.concatenate(outs[0:4], 0), np.concatenate(outs[4:8], 0)], 0)


if __name__ == "__main__":
    nc = build_program()
    n = sum(1 for _ in nc.all_instructions())
    print("instructions:", n)


# version 05
# speedup vs baseline: 1.0708x; 1.0708x over previous
import numpy as np
import ml_dtypes
from contextlib import ExitStack
import concourse.bass as bass
import concourse.mybir as mybir
from concourse.bass_utils import run_bass_kernel_spmd

F32 = mybir.dt.float32
BF16 = mybir.dt.bfloat16
AF = mybir.ActivationFunctionType
ALU = mybir.AluOpType
AX = mybir.AxisListType.X

B, S, D = 2, 4096, 2048
H, DH = 16, 128
HID1, HID2 = 5461, 8192
T = 1024
NT = T // 128
NK = D // 128
NPRE = 3
EPS = 1e-5
SCL = float(DH) ** -0.5

ENG = ("sync", "scalar", "vector", "tensor", "gpsimd")


def ts(i, n):
    return slice(i * n, (i + 1) * n)


DMA_LANES = 4


class P:
    def __init__(self, nc):
        self.nc = nc
        self.ops = {e: [] for e in ENG}
        self.sem = {}
        self.cnt = {}
        self.hw = {e: {} for e in ENG}
        self.dma_i = {"ss": 0, "sa": 0, "sq": 0}

    def add_sem(self, name, handle):
        self.sem[name] = handle
        self.cnt[name] = 0

    def wait(self, eng, dep):
        if dep is None:
            return
        s, v = dep
        if v <= 0:
            return
        if self.hw[eng].get(s, 0) >= v:
            return
        self.hw[eng][s] = v
        h = self.sem[s]
        self.ops[eng].append(lambda E, h=h, v=v: E.wait_ge(h, v))

    def waits(self, eng, deps):
        for d in deps:
            self.wait(eng, d)

    def fence_all(self):
        return [(s, self.cnt[s]) for s in self.sem]

    def dma(self, eng, out, in_, waits=(), transpose=False):
        base = {"sync": "ss", "scalar": "sa", "gpsimd": "sq"}[eng]
        i = self.dma_i[base]
        self.dma_i[base] = i + 1
        lane = f"{base}{i % DMA_LANES}"
        self.wait(eng, (lane, 16 * (i // DMA_LANES)))
        self.waits(eng, waits)
        v = self.cnt[lane] = 16 * (i // DMA_LANES + 1)
        h = self.sem[lane]

        def f(E, out=out, in_=in_, h=h, transpose=transpose):
            E.dma_start(out=out, in_=in_, transpose=transpose).then_inc(h, 16)

        self.ops[eng].append(f)
        return (lane, v)

    def op(self, eng, fn, waits=(), inc=True):
        sname = {"tensor": "pt", "vector": "pv", "scalar": "pc", "gpsimd": "pg"}[eng]
        if eng != "tensor":
            inc = True
            self.wait(eng, (sname, self.cnt[sname]))
        self.waits(eng, waits)
        if inc:
            v = self.cnt[sname] = self.cnt[sname] + 1
            h = self.sem[sname]

            def f(E, fn=fn, h=h):
                fn(E).then_inc(h, 1)

            self.ops[eng].append(f)
            return (sname, v)
        self.ops[eng].append(lambda E, fn=fn: fn(E))
        return None


def build_program():
    nc = bass.Bass("TRN2", target_bir_lowering=False)

    di = lambda n, sh, dt=F32: nc.dram_tensor(n, sh, dt, kind="ExternalInput").ap()
    x = di("x", [T, D])
    xpre = di("xpre", [NPRE * T, D])
    wq, wk, wv, wg, wo = (di(n, [D, D]) for n in ("wq", "wk", "wv", "wg", "wo"))
    w1, w2 = di("w1", [D, HID1]), di("w2", [D, HID1])
    w3 = di("w3", [HID1, D])
    wgate, wup = di("wgate", [HID2, D]), di("wup", [HID2, D])
    wdown = di("wdown", [D, HID2])
    dec = di("dec", [H, 8, 128, 512])
    gqrow = di("gqrow", [1, H * T], BF16)
    kgc = di("kgc", [128, NPRE * NT * H])
    n1w, n1b = di("n1w", [1, D]), di("n1b", [1, D])
    n2w, n2b = di("n2w", [1, D]), di("n2b", [1, D])
    out = nc.dram_tensor("out", [T, D], F32, kind="ExternalOutput").ap()

    dd = lambda n, sh, dt: nc.dram_tensor(n, sh, dt).ap()
    s1 = dd("s1", [T, D], BF16)
    x2d = dd("x2d", [T, D], F32)
    xBd = dd("xBd", [T, D], F32)
    wgate_bf = dd("wgate_bf", [HID2, D], BF16)
    wup_bf = dd("wup_bf", [HID2, D], BF16)
    wdown_bf = dd("wdown_bf", [D, HID2], BF16)
    wdownT = dd("wdownT", [HID2, D], BF16)

    with ExitStack() as st:
        e = st.enter_context
        p = P(nc)
        for base in ("ss", "sa", "sq"):
            for l in range(DMA_LANES):
                p.add_sem(f"{base}{l}", e(nc.semaphore(f"{base}{l}")))
        for sname in ("pt", "pv", "pc", "pg"):
            p.add_sem(sname, e(nc.semaphore(sname)))

        arena = e(nc.sbuf_tensor("arena", [128, NK * T], BF16, is_transpose_dest=True))
        stats = e(nc.sbuf_tensor("stats", [128, 64], F32))
        kgc_s = e(nc.sbuf_tensor("kgc_s", [128, NPRE * NT * H], F32))
        ones1 = e(nc.sbuf_tensor("ones1", [1, 128], BF16))
        onesS = e(nc.sbuf_tensor("onesS", [128, 128], F32))
        gq_s = e(nc.sbuf_tensor("gq_s", [1, T], BF16))
        epsb = e(nc.sbuf_tensor("epsb", [128, 1], F32))
        eps2b = e(nc.sbuf_tensor("eps2b", [128, 1], F32))
        pb = [e(nc.psum_tensor(f"pb{i}", [128, 512], F32)) for i in range(8)]
        bank_free = [None] * 8

        p.op("vector", lambda E: E.memset(ones1[:, :], 1.0), inc=False)
        p.op("vector", lambda E: E.memset(epsb[:, :], EPS), inc=False)
        p.op("vector", lambda E: E.memset(eps2b[:, :], EPS * EPS), inc=False)
        d_ones = p.op("vector", lambda E: E.memset(onesS[:, :], 1.0))
        d_kgc = p.dma("sync", kgc_s[:, :], kgc)

        d_gbf = [p.dma("gpsimd", wgate_bf[ts(i, 2048), :], wgate[ts(i, 2048), :])
                 for i in range(4)]
        d_ubf = [p.dma("gpsimd", wup_bf[ts(i, 2048), :], wup[ts(i, 2048), :])
                 for i in range(4)]
        d_dbf = [p.dma("gpsimd", wdown_bf[ts(i, 512), :], wdown[ts(i, 512), :])
                 for i in range(4)]

        s1_free = [[]]

        def wload(eng, dst_ap, src_ap, n, waits=()):
            return p.dma(eng, dst_ap.rearrange("p (a n) -> p a n", n=n),
                         src_ap.rearrange("(a p) n -> p a n", p=128), waits=waits)

        def replicate_rows(dst, dep):
            deps = []
            for j in range(4):
                dm = p.op("tensor", lambda E, j=j, dst=dst: E.matmul(
                    pb[7][:, 0:512], onesS[0:1, :], dst[0:1, ts(j, 512)],
                    start=True, stop=True), waits=[dep, d_ones, bank_free[7]])
                dc = p.op("vector", lambda E, j=j, dst=dst: E.tensor_copy(
                    dst[:, ts(j, 512)], pb[7][:, 0:512]), waits=[dm])
                bank_free[7] = dc
                deps.append(dc)
            return deps

        _uid = [0]

        def ln_block(src_ap, dst_arena, affine, col0):
            fence = p.fence_all()
            _uid[0] += 1
            u = _uid[0]
            with ExitStack() as ls:
                le = ls.enter_context
                xt = le(nc.sbuf_tensor(f"ln_xt{u}", [128, 2 * 2048], F32))
                sq = le(nc.sbuf_tensor(f"ln_sq{u}", [128, 2048], F32))
                tst = le(nc.sbuf_tensor(f"ln_tst{u}", [128, 16], F32))
                ht = le(nc.sbuf_tensor(f"ln_ht{u}", [128, 2 * 2048], BF16))
                nrm = sq
                if affine is not None:
                    wrep = le(nc.sbuf_tensor(f"ln_wrep{u}", [128, 2048], F32))
                    brep = le(nc.sbuf_tensor(f"ln_brep{u}", [128, 2048], F32))
                    dw_ = p.dma("sync", wrep[0:1, :], affine[1], waits=fence)
                    db_ = p.dma("sync", brep[0:1, :], affine[2], waits=fence)
                    rep = replicate_rows(wrep, dw_) + replicate_rows(brep, db_)
                else:
                    rep = []
                st_deps = []
                prev_cons = [None, None]
                prev_stage = [None, None]
                sq_free = None
                for t in range(NT):
                    sl = ts(t % 2, 2048)
                    dld = p.dma("sync", xt[:, sl], src_ap[ts(t, 128), :],
                                waits=[prev_cons[t % 2]] + rep + fence)
                    dsum = p.op("vector", lambda E, sl=sl: E.reduce_sum(
                        tst[:, 0:1], xt[:, sl], axis=AX), waits=[dld] + fence)
                    dsq = p.op("scalar", lambda E, sl=sl: E.activation(
                        sq[:, :], xt[:, sl], AF.Square), waits=[dld, sq_free] + fence)
                    dsum2 = p.op("vector", lambda E: E.reduce_sum(
                        tst[:, 1:2], sq[:, :], axis=AX), waits=[dsq])
                    sq_free = dsum2
                    cn, cr = col0 + 2 * t, col0 + 2 * t + 1
                    p.op("vector", lambda E, cn=cn: E.tensor_scalar_mul(
                        stats[:, cn:cn + 1], tst[:, 0:1], -1.0 / D),
                        waits=[dsum], inc=False)
                    p.op("vector", lambda E: E.tensor_scalar_mul(
                        tst[:, 2:3], tst[:, 1:2], 1.0 / D), waits=[dsum2], inc=False)
                    p.op("vector", lambda E, cn=cn: E.tensor_mul(
                        tst[:, 3:4], stats[:, cn:cn + 1], stats[:, cn:cn + 1]),
                        inc=False)
                    dvar = p.op("vector", lambda E: E.tensor_sub(
                        tst[:, 4:5], tst[:, 2:3], tst[:, 3:4]))
                    dln_ = p.op("scalar", lambda E: E.activation(
                        tst[:, 5:6], tst[:, 4:5], AF.Ln, bias=epsb[:, 0:1]), waits=[dvar])
                    drs = p.op("scalar", lambda E, cr=cr: E.activation(
                        stats[:, cr:cr + 1], tst[:, 5:6], AF.Exp, scale=-0.5),
                        waits=[dln_])
                    if affine is not None and affine[0] == "combined":
                        cc1 = col0 + 16 + t
                        dl2 = p.op("scalar", lambda E: E.activation(
                            tst[:, 9:10], tst[:, 4:5], AF.Ln,
                            bias=eps2b[:, 0:1], scale=1.0 + EPS), waits=[dvar])
                        dsc = p.op("scalar", lambda E, cc1=cc1: E.activation(
                            stats[:, cc1:cc1 + 1], tst[:, 9:10], AF.Exp,
                            scale=-0.5), waits=[dl2])
                        scol = cc1
                    else:
                        dsc, scol = drs, cr
                    hsl = ts(t % 2, 2048)
                    if affine is None:
                        dn = p.op("vector", lambda E, sl=sl, hsl=hsl, cn=cn, scol=scol:
                                  E.tensor_scalar(ht[:, hsl], xt[:, sl],
                                                  stats[:, cn:cn + 1],
                                                  stats[:, scol:scol + 1],
                                                  ALU.add, ALU.mult),
                                  waits=[dsc, prev_stage[t % 2]])
                    else:
                        d0 = p.op("vector", lambda E, sl=sl, cn=cn, scol=scol:
                                  E.tensor_scalar(nrm[:, :], xt[:, sl],
                                                  stats[:, cn:cn + 1],
                                                  stats[:, scol:scol + 1],
                                                  ALU.add, ALU.mult),
                                  waits=[dsc])
                        d1 = p.op("vector", lambda E: E.tensor_mul(
                            nrm[:, :], nrm[:, :], wrep[:, :]), waits=[d0])
                        dn = p.op("vector", lambda E, hsl=hsl: E.tensor_add(
                            ht[:, hsl], nrm[:, :], brep[:, :]),
                            waits=[d1, prev_stage[t % 2]])
                    prev_cons[t % 2] = dn
                    sq_free = dn
                    dst_ = p.dma("sync", s1[ts(t, 128), :], ht[:, hsl],
                                 waits=[dn] + s1_free[0])
                    prev_stage[t % 2] = dst_
                    st_deps.append(dst_)
                tr_deps = []
                pe_now = ("pt", p.cnt["pt"])
                for k in range(NK):
                    dtr = p.dma("scalar", dst_arena[:, ts(k, T)], s1[:, ts(k, 128)],
                                waits=st_deps + [pe_now], transpose=True)
                    tr_deps.append(dtr)
                s1_free[0] = tr_deps
                return tr_deps

        with ExitStack() as hkv_st:
            he = hkv_st.enter_context
            h_f = he(nc.sbuf_tensor("h_f", [128, NK * T], BF16, is_transpose_dest=True))
            r_bf = he(nc.sbuf_tensor("r_bf", [128, H * 128], BF16))

            tr_h = ln_block(x, h_f, ("combined", n1w, n1b), 0)

            with ExitStack() as pre_st:
                r_acc = pre_st.enter_context(
                    nc.sbuf_tensor("r_acc", [128, H * 128], F32))
                d_racc0 = p.op("vector", lambda E: E.memset(r_acc[:, :], 0.0))
                for c in range(NPRE):
                    tr_hp = ln_block(xpre[ts(c, T), :], arena,
                                     ("combined", n1w, n1b), 32)
                    fence = p.fence_all()
                    with ExitStack() as ch_st:
                        ce = ch_st.enter_context
                        kw_sb = ce(nc.sbuf_tensor(f"kw{c}", [128, NT * D], BF16))
                        vw_sb = ce(nc.sbuf_tensor(f"vw{c}", [128, NT * D], BF16))
                        wslab = ce(nc.sbuf_tensor(f"wslab{c}", [128, 2 * NK * 512], BF16))
                        w_dep = [None, None]
                        last_rd = [None, None]
                        ev_last = None
                        for s4 in range(4):
                            for kv in range(2):
                                buf = (2 * s4 + kv) % 2
                                off = buf * NK * 512
                                wsrc = wk if kv == 0 else wv
                                w_dep[buf] = wload("gpsimd", wslab[:, off:off + NK * 512],
                                                   wsrc[:, ts(s4, 512)], 512,
                                                   waits=[w_dep[buf], last_rd[buf]] + fence)
                                for t in range(NT):
                                    bk = pb[t % 4]
                                    p.waits("tensor",
                                            [w_dep[buf], bank_free[t % 4]] + tr_hp)
                                    for k in range(NK):
                                        def f(E, k=k, t=t, bk=bk, off=off):
                                            return E.matmul(
                                                bk[:, 0:512],
                                                arena[:, k * T + t * 128: k * T + t * 128 + 128],
                                                wslab[:, off + k * 512: off + (k + 1) * 512],
                                                start=(k == 0), stop=(k == NK - 1))
                                        dmm = p.op("tensor", f, inc=(k == NK - 1))
                                    if kv == 0:
                                        evs = []
                                        for hh in range(4):
                                            head = 4 * s4 + hh
                                            kc = (c * NT + t) * H + head
                                            ev = p.op(
                                                "vector",
                                                lambda E, t=t, hh=hh, head=head, kc=kc, bk=bk:
                                                E.tensor_scalar_mul(
                                                    kw_sb[:, t * D + head * 128: t * D + (head + 1) * 128],
                                                    bk[:, hh * 128:(hh + 1) * 128],
                                                    kgc_s[:, kc:kc + 1]),
                                                waits=[dmm, d_kgc] + fence)
                                            evs.append(ev)
                                        bank_free[t % 4] = evs[-1]
                                        ev_last = evs[-1]
                                    else:
                                        ev = p.op(
                                            "vector",
                                            lambda E, t=t, s4=s4, bk=bk:
                                            E.tensor_copy(
                                                vw_sb[:, t * D + s4 * 512: t * D + (s4 + 1) * 512],
                                                bk[:, 0:512]),
                                            waits=[dmm] + fence)
                                        bank_free[t % 4] = ev
                                        ev_last = ev
                                last_rd[buf] = dmm
                        for q in range(4):
                            bq = pb[4 + q]
                            p.waits("tensor", [ev_last, bank_free[4 + q]])
                            for hh in range(4):
                                head = 4 * q + hh
                                for t in range(NT):
                                    def f(E, t=t, hh=hh, head=head, bq=bq):
                                        return E.matmul(
                                            bq[:, hh * 128:(hh + 1) * 128],
                                            kw_sb[:, t * D + head * 128: t * D + (head + 1) * 128],
                                            vw_sb[:, t * D + head * 128: t * D + (head + 1) * 128],
                                            start=(t == 0), stop=(t == NT - 1))
                                    dmr = p.op("tensor", f, inc=(t == NT - 1))
                            dadd = p.op("vector", lambda E, q=q, bq=bq: E.tensor_add(
                                r_acc[:, ts(q, 512)], r_acc[:, ts(q, 512)],
                                bq[:, 0:512]), waits=[dmr, d_racc0])
                            bank_free[4 + q] = dadd
                d_rbf = p.op("vector", lambda E: E.tensor_copy(
                    r_bf[:, :], r_acc[:, :]))

            k_f = he(nc.sbuf_tensor("k_f", [128, H * T], BF16))
            v_T = he(nc.sbuf_tensor("v_T", [128, NT * D], BF16))
            wcb = he(nc.sbuf_tensor("wcb", [128, 2 * 2048], BF16))
            wcb_free = [None, None]
            m_st = ExitStack()
            wslab = m_st.enter_context(
                nc.sbuf_tensor("wslab_m", [128, 2 * 8192], BF16))
            fence = p.fence_all()
            wv_dep = [None, None]
            for s in range(4):
                buf = s % 2
                vo = buf * 8192
                wv_dep[buf] = wload(
                    "gpsimd", wslab[:, vo:vo + 8192],
                    wv[:, ts(s, 512)], 512,
                    waits=[wv_dep[buf]] + fence)
                ev = None
                for t in range(NT):
                    bv = pb[t % 4]
                    p.waits("tensor", [wv_dep[buf], bank_free[t % 4]] + tr_h)
                    for k in range(NK):
                        def f(E, k=k, t=t, bv=bv, vo=vo):
                            return E.matmul(
                                bv[:, 0:512],
                                h_f[:, k * T + t * 128: k * T + t * 128 + 128],
                                wslab[:, vo + k * 512: vo + (k + 1) * 512],
                                start=(k == 0), stop=(k == NK - 1))
                        dmv = p.op("tensor", f, inc=(k == NK - 1))
                    ev = p.op("vector", lambda E, s=s, t=t, bv=bv: E.tensor_copy(
                        v_T[:, t * 2048 + s * 512: t * 2048 + (s + 1) * 512],
                        bv[:, 0:512]), waits=[dmv])
                    bank_free[t % 4] = ev
                wv_dep[buf] = ev

            fence_m = p.fence_all()

            def feat_proj(wdram, head, evac_cb, banks=(0, 1)):
                buf = head % 2
                off = buf * 2048
                wdep = wload("gpsimd", wcb[:, off:off + 2048],
                             wdram[:, ts(head, 128)], 128,
                             waits=[wcb_free[buf]] + fence_m)
                outs = []
                last_mm = None
                for half in range(2):
                    bk = pb[banks[half]]
                    p.waits("tensor", [wdep, bank_free[banks[half]]] + tr_h)
                    for k in range(NK):
                        def f(E, k=k, half=half, bk=bk, off=off):
                            return E.matmul(
                                bk[:, 0:512],
                                wcb[:, off + k * 128: off + (k + 1) * 128],
                                h_f[:, k * T + half * 512: k * T + (half + 1) * 512],
                                start=(k == 0), stop=(k == NK - 1))
                        dmm = p.op("tensor", f, inc=(k == NK - 1))
                    last_mm = dmm
                    ev = evac_cb(half, bk, dmm)
                    bank_free[banks[half]] = ev
                    outs.append(ev)
                wcb_free[buf] = last_mm
                return outs

            for head in range(H):
                def evk(half, bk, gdep, head=head):
                    return p.op("vector", lambda E, half=half, bk=bk, head=head:
                                E.tensor_scalar_mul(
                                    k_f[:, head * T + half * 512: head * T + (half + 1) * 512],
                                    bk[:, 0:512], SCL), waits=[gdep])
                feat_proj(wk, head, evk)
            m_st.close()

            with ExitStack() as rs:
                re_ = rs.enter_context
                qb = re_(nc.sbuf_tensor("qb", [128, T], BF16))
                qg = re_(nc.sbuf_tensor("qg", [128, T], BF16))
                sbf = re_(nc.sbuf_tensor("sbf", [128, 12 * 512], BF16))
                decb = re_(nc.sbuf_tensor("decb", [128, 3 * 512], F32))
                retsb = re_(nc.sbuf_tensor("retsb", [128, T], F32))
                gnt = re_(nc.sbuf_tensor("gnt", [128, T], F32))
                negmb = re_(nc.sbuf_tensor("negmb", [128, T], F32))
                rstdb = re_(nc.sbuf_tensor("rstdb", [128, T], F32))
                gsb = re_(nc.sbuf_tensor("gsb", [128, T], F32))
                fence = p.fence_all()

                pairs = [(0, 0), (0, 1), (0, 2), (0, 3),
                         (1, 0), (1, 1), (1, 2), (1, 3),
                         (1, 4), (1, 5), (1, 6), (1, 7)]
                dec_slot = [None, None, None]
                q_cons = None
                g_cons = None
                gq_cons = None
                head_done = None
                sbf_free = None
                for head in range(H):
                    def evq(half, bk, gdep):
                        return p.op("vector", lambda E, half=half, bk=bk:
                                    E.tensor_copy(qb[:, ts(half, 512)],
                                                  bk[:, 0:512]),
                                    waits=[gdep, q_cons] + fence)
                    qouts = feat_proj(wq, head, evq)

                    dgq = p.dma("sync", gq_s[0:1, :], gqrow[0:1, ts(head, T)],
                                waits=[gq_cons])
                    qg_deps = []
                    for half in range(2):
                        p.waits("tensor", [dgq, qouts[half], bank_free[6]])
                        dbc = p.op("tensor", lambda E, half=half: E.matmul(
                            pb[6][:, 0:512], ones1[0:1, :],
                            gq_s[0:1, ts(half, 512)], start=True, stop=True))
                        dqg = p.op("vector", lambda E, half=half: E.tensor_mul(
                            qg[:, ts(half, 512)], qb[:, ts(half, 512)],
                            pb[6][:, 0:512]), waits=[dbc])
                        bank_free[6] = dqg
                        qg_deps.append(dqg)
                    gq_cons = qg_deps[-1]

                    sc_deps = {}
                    for i, (qt, kt) in enumerate(pairs):
                        r = 4 * qt - kt + 3
                        ddec = p.dma(
                            "scalar", decb[:, ts(i % 3, 512)], dec[head, r, :, :],
                            waits=[dec_slot[i % 3]] + fence)
                        bk = pb[2 + (i % 2)]
                        p.waits("tensor", [qouts[qt], bank_free[2 + (i % 2)]])
                        dsc = p.op("tensor", lambda E, qt=qt, kt=kt, bk=bk, head=head:
                                   E.matmul(
                                       bk[:, 0:512],
                                       k_f[:, head * T + kt * 128: head * T + kt * 128 + 128],
                                       qb[:, ts(qt, 512)], start=True, stop=True))
                        slot = kt if qt == 0 else 4 + kt
                        dsb = p.op("vector", lambda E, slot=slot, i=i, bk=bk:
                                   E.tensor_mul(sbf[:, ts(slot, 512)], bk[:, 0:512],
                                                decb[:, ts(i % 3, 512)]),
                                   waits=[dsc, ddec, sbf_free])
                        dec_slot[i % 3] = dsb
                        bank_free[2 + (i % 2)] = dsb
                        sc_deps[(qt, kt)] = dsb

                    ret_deps = []
                    last_ret_mm = None
                    for qt in range(2):
                        bk = pb[4 + qt]
                        nkt = 4 if qt == 0 else 8
                        p.waits("tensor", [qg_deps[qt], d_rbf, bank_free[4 + qt]])
                        p.op("tensor", lambda E, qt=qt, bk=bk, head=head: E.matmul(
                            bk[:, 0:512], r_bf[:, ts(head, 128)],
                            qg[:, ts(qt, 512)], start=True, stop=False), inc=False)
                        for kt in range(nkt):
                            slot = kt if qt == 0 else 4 + kt
                            p.waits("tensor", [sc_deps[(qt, kt)]])

                            def f(E, qt=qt, kt=kt, slot=slot, bk=bk, head=head):
                                return E.matmul(
                                    bk[:, 0:512],
                                    v_T[:, kt * 2048 + head * 128: kt * 2048 + (head + 1) * 128],
                                    sbf[:, ts(slot, 512)],
                                    start=False, stop=(kt == nkt - 1))
                            dmm = p.op("tensor", f, inc=(kt == nkt - 1))
                        last_ret_mm = dmm
                        dev = p.op("vector", lambda E, qt=qt, bk=bk: E.tensor_copy(
                            retsb[:, ts(qt, 512)], bk[:, 0:512]),
                            waits=[dmm, head_done])
                        bank_free[4 + qt] = dev
                        ret_deps.append(dev)
                    sbf_free = last_ret_mm

                    dsq = p.op("scalar", lambda E: E.activation(
                        gnt[:, :], retsb[:, :], AF.Square),
                        waits=ret_deps + [head_done] + fence)
                    stat_deps = []
                    for half in range(2):
                        p.waits("tensor", ret_deps + [bank_free[6]])
                        dm = p.op("tensor", lambda E, half=half: E.matmul(
                            pb[6][:, 0:512], onesS[:, :], retsb[:, ts(half, 512)],
                            start=True, stop=True))
                        p.waits("tensor", [dsq, bank_free[7]])
                        ds2 = p.op("tensor", lambda E, half=half: E.matmul(
                            pb[7][:, 0:512], onesS[:, :], gnt[:, ts(half, 512)],
                            start=True, stop=True))
                        dnm = p.op("vector", lambda E, half=half:
                                   E.tensor_scalar_mul(negmb[:, ts(half, 512)],
                                                       pb[6][:, 0:512], -1.0 / 128),
                                   waits=[dm])
                        bank_free[6] = dnm
                        dm2 = p.op("vector", lambda E, half=half: E.tensor_mul(
                            gnt[:, ts(half, 512)], negmb[:, ts(half, 512)],
                            negmb[:, ts(half, 512)]), waits=[dnm, dsq])
                        dvr = p.op("vector", lambda E, half=half:
                                   E.scalar_tensor_tensor(
                                       gnt[:, ts(half, 512)], pb[7][:, 0:512],
                                       1.0 / 128, gnt[:, ts(half, 512)],
                                       ALU.mult, ALU.subtract),
                                   waits=[ds2, dm2])
                        bank_free[7] = dvr
                        dl = p.op("scalar", lambda E, half=half: E.activation(
                            rstdb[:, ts(half, 512)], gnt[:, ts(half, 512)],
                            AF.Ln, bias=epsb[:, 0:1]), waits=[dvr])
                        dr = p.op("scalar", lambda E, half=half: E.activation(
                            rstdb[:, ts(half, 512)], rstdb[:, ts(half, 512)],
                            AF.Exp, scale=-0.5), waits=[dl])
                        stat_deps.append(dr)

                    def evg(half, bk, gdep):
                        return p.op("scalar", lambda E, half=half, bk=bk:
                                    E.activation(gsb[:, ts(half, 512)],
                                                 bk[:, 0:512], AF.Silu),
                                    waits=[gdep, g_cons] + fence)
                    gouts = feat_proj(wg, head, evg, banks=(2, 3))
                    g_cons = gouts[-1]

                    a_deps = []
                    for half in range(2):
                        d1 = p.op("vector", lambda E, half=half: E.tensor_add(
                            retsb[:, ts(half, 512)], retsb[:, ts(half, 512)],
                            negmb[:, ts(half, 512)]), waits=stat_deps + ret_deps)
                        d2 = p.op("vector", lambda E, half=half: E.tensor_mul(
                            retsb[:, ts(half, 512)], retsb[:, ts(half, 512)],
                            rstdb[:, ts(half, 512)]), waits=[d1])
                        d3 = p.op("vector", lambda E, half=half, head=head:
                                  E.tensor_mul(
                                      arena[:, head * T + half * 512: head * T + (half + 1) * 512],
                                      retsb[:, ts(half, 512)],
                                      gsb[:, ts(half, 512)]),
                                  waits=[d2, gouts[half]])
                        a_deps.append(d3)
                    q_cons = a_deps[-1]
                    head_done = a_deps[-1]
                a_done = head_done

        with ExitStack() as ys:
            ye = ys.enter_context
            woq = ye(nc.sbuf_tensor("woq", [128, 2 * 8192], BF16))
            yx = ye(nc.sbuf_tensor("yx", [128, 2 * 512], F32))
            yst = ye(nc.sbuf_tensor("yst", [128, 2 * 512], F32))
            tstg = ye(nc.sbuf_tensor("tstg", [128, 4 * 2048], BF16,
                                     is_transpose_dest=True))
            fence = p.fence_all()

            wo_dep = [None, None]
            yx_dep = [None, None]
            yst_dep = [None, None]
            x2_last = None
            last_y_mm = [None, None]
            for s in range(4):
                buf = s % 2
                wo_dep[buf] = wload(
                    "gpsimd", woq[:, ts(buf, 8192)],
                    wo[:, ts(s, 512)], 512,
                    waits=[wo_dep[buf], last_y_mm[buf]] + fence)
                for t in range(NT):
                    bk = pb[t % 2]
                    p.waits("tensor", [wo_dep[buf], a_done, bank_free[t % 2]])
                    for k in range(NK):
                        def f(E, k=k, t=t, bk=bk, buf=buf):
                            return E.matmul(
                                bk[:, 0:512],
                                arena[:, k * T + t * 128: k * T + t * 128 + 128],
                                woq[:, buf * 8192 + k * 512: buf * 8192 + (k + 1) * 512],
                                start=(k == 0), stop=(k == NK - 1))
                        dmm = p.op("tensor", f, inc=(k == NK - 1))
                    last_y_mm[buf] = dmm
                    xb = t % 2
                    dx = p.dma("sync", yx[:, ts(xb, 512)], x[ts(t, 128), ts(s, 512)],
                               waits=[yx_dep[xb]] + fence)
                    cn = 2 * t
                    d0 = p.op("vector", lambda E, xb=xb, cn=cn: E.tensor_scalar(
                        yst[:, ts(xb, 512)], yx[:, ts(xb, 512)],
                        stats[:, cn:cn + 1], stats[:, cn + 1:cn + 2],
                        ALU.add, ALU.mult),
                        waits=[dx, yst_dep[xb]] + fence)
                    yx_dep[xb] = d0
                    d1 = p.op("vector", lambda E, xb=xb, bk=bk: E.tensor_add(
                        yst[:, ts(xb, 512)], yst[:, ts(xb, 512)], bk[:, 0:512]),
                        waits=[d0, dmm])
                    bank_free[t % 2] = d1
                    dst_ = p.dma("sync", x2d[ts(t, 128), ts(s, 512)],
                                 yst[:, ts(xb, 512)], waits=[d1])
                    yst_dep[xb] = dst_
                    x2_last = dst_

            tstg_dep = [None, None]
            dT_last = None
            for i in range(HID2 // 128):
                slot = i % 2
                dtr = p.dma("scalar", tstg[:, slot * 2048: slot * 2048 + 2048],
                            wdown_bf[0:D, ts(i, 128)],
                            waits=[tstg_dep[slot]] + d_dbf + fence, transpose=True)
                dst_ = p.dma("sync", wdownT[ts(i, 128), 0:D],
                             tstg[:, slot * 2048: slot * 2048 + 2048], waits=[dtr])
                tstg_dep[slot] = dst_
                dT_last = dst_

        tr_h2 = ln_block(x2d, arena, ("plain-affine", n2w, n2b), 32)

        with ExitStack() as f1s:
            fe = f1s.enter_context
            a2 = fe(nc.sbuf_tensor("a2", [128, 43 * 1024], BF16))
            w12cb = fe(nc.sbuf_tensor("w12cb", [128, 4 * 2048], BF16))
            silu_t = fe(nc.sbuf_tensor("silu_t", [128, 2 * 512], F32))
            fence = p.fence_all()

            w12_dep = [None, None, None, None]
            silu_dep = [None, None]
            a2_last = None
            for ch in range(43):
                ncc = 128 if ch < 42 else 85
                buf = ch % 2
                dw1 = wload("gpsimd", w12cb[:, buf * 2048: buf * 2048 + NK * ncc],
                            w1[:, ch * 128: ch * 128 + ncc], ncc,
                            waits=[w12_dep[buf]] + fence)
                dw2 = wload("gpsimd",
                            w12cb[:, (2 + buf) * 2048: (2 + buf) * 2048 + NK * ncc],
                            w2[:, ch * 128: ch * 128 + ncc], ncc,
                            waits=[w12_dep[2 + buf]] + fence)
                ch_dep = None
                for half in range(2):
                    b1 = pb[half]
                    b2_ = pb[2 + half]
                    p.waits("tensor", [dw1, bank_free[half]] + tr_h2)
                    for k in range(NK):
                        def f(E, k=k, half=half, b1=b1, buf=buf, ncc=ncc):
                            return E.matmul(
                                b1[0:ncc, 0:512],
                                w12cb[:, buf * 2048 + k * ncc: buf * 2048 + (k + 1) * ncc],
                                arena[:, k * T + half * 512: k * T + (half + 1) * 512],
                                start=(k == 0), stop=(k == NK - 1))
                        du1 = p.op("tensor", f, inc=(k == NK - 1))
                    p.waits("tensor", [dw2, bank_free[2 + half]])
                    for k in range(NK):
                        def f(E, k=k, half=half, b2_=b2_, buf=buf, ncc=ncc):
                            return E.matmul(
                                b2_[0:ncc, 0:512],
                                w12cb[:, (2 + buf) * 2048 + k * ncc: (2 + buf) * 2048 + (k + 1) * ncc],
                                arena[:, k * T + half * 512: k * T + (half + 1) * 512],
                                start=(k == 0), stop=(k == NK - 1))
                        du2 = p.op("tensor", f, inc=(k == NK - 1))
                    dsl = p.op("scalar", lambda E, half=half, b1=b1, ncc=ncc:
                               E.activation(silu_t[0:ncc, ts(half, 512)],
                                            b1[0:ncc, 0:512], AF.Silu),
                               waits=[du1, silu_dep[half]] + fence)
                    bank_free[half] = dsl
                    da2 = p.op("vector",
                               lambda E, ch=ch, half=half, b2_=b2_, ncc=ncc:
                               E.tensor_mul(
                                   a2[0:ncc, ch * 1024 + half * 512: ch * 1024 + (half + 1) * 512],
                                   silu_t[0:ncc, ts(half, 512)],
                                   b2_[0:ncc, 0:512]),
                               waits=[dsl, du2] + fence)
                    bank_free[2 + half] = da2
                    silu_dep[half] = da2
                    ch_dep = da2
                w12_dep[buf] = ch_dep
                w12_dep[2 + buf] = ch_dep
                a2_last = ch_dep

            with ExitStack() as f1b:
                fb = f1b.enter_context
                w3b = fb(nc.sbuf_tensor("w3b", [128, 2 * 11008], BF16))
                fxt = fb(nc.sbuf_tensor("fxt", [128, 4 * 256], F32))
                fst = fb(nc.sbuf_tensor("fst", [128, 2 * 256], F32))
                fence = p.fence_all()
                w3_dep = [None, None]
                fx_dep = [None, None, None, None]
                fst_dep = [None, None]
                xB_last = None
                last_f_mm = [None, None]
                for s8 in range(8):
                    buf = s8 % 2
                    off = buf * 11008
                    dw3a = wload("gpsimd", w3b[:, off: off + 42 * 256],
                                 w3[0:5376, ts(s8, 256)], 256,
                                 waits=[w3_dep[buf], last_f_mm[buf]] + fence)
                    dw3b = p.dma("gpsimd", w3b[0:85, off + 42 * 256: off + 43 * 256],
                                 w3[5376:5461, ts(s8, 256)])
                    for t in range(NT):
                        bk = pb[4 + t % 2]
                        p.waits("tensor", [dw3a, dw3b, bank_free[4 + t % 2], a2_last])
                        for ch in range(43):
                            ncc = 128 if ch < 42 else 85

                            def f(E, ch=ch, t=t, bk=bk, off=off, ncc=ncc):
                                return E.matmul(
                                    bk[:, 0:256],
                                    a2[0:ncc, ch * 1024 + t * 128: ch * 1024 + t * 128 + 128],
                                    w3b[0:ncc, off + ch * 256: off + (ch + 1) * 256],
                                    start=(ch == 0), stop=(ch == 42))
                            dmm = p.op("tensor", f, inc=(ch == 42))
                        last_f_mm[buf] = dmm
                        xb = t % 2
                        dx2 = p.dma("sync", fxt[:, ts(xb, 256)],
                                    x2d[ts(t, 128), ts(s8, 256)],
                                    waits=[fx_dep[xb]] + fence)
                        dxr = p.dma("sync", fxt[:, ts(2 + xb, 256)],
                                    x[ts(t, 128), ts(s8, 256)],
                                    waits=[fx_dep[2 + xb]] + fence)
                        d0 = p.op("vector", lambda E, xb=xb, bk=bk: E.tensor_add(
                            fst[:, ts(xb, 256)], fxt[:, ts(xb, 256)], bk[:, 0:256]),
                            waits=[dx2, dmm, fst_dep[xb]] + fence)
                        bank_free[4 + t % 2] = d0
                        fx_dep[xb] = d0
                        d1 = p.op("vector", lambda E, xb=xb: E.tensor_add(
                            fst[:, ts(xb, 256)], fst[:, ts(xb, 256)],
                            fxt[:, ts(2 + xb, 256)]), waits=[dxr, d0])
                        fx_dep[2 + xb] = d1
                        dst_ = p.dma("sync", xBd[ts(t, 128), ts(s8, 256)],
                                     fst[:, ts(xb, 256)], waits=[d1])
                        fst_dep[xb] = dst_
                        xB_last = dst_
                    w3_dep[buf] = xB_last

        tr_xb = ln_block(xBd, arena, None, 32)

        with ExitStack() as f2s:
            fe = f2s.enter_context
            a3 = fe(nc.sbuf_tensor("a3", [128, 64 * 512], BF16))
            wwin = fe(nc.sbuf_tensor("wwin", [128, 2 * 16384], BF16,
                                     is_transpose_dest=True))
            f2x = fe(nc.sbuf_tensor("f2x", [128, 2 * 512], F32))
            f2st = fe(nc.sbuf_tensor("f2st", [128, 2 * 512], F32))
            silu2 = fe(nc.sbuf_tensor("silu2", [128, 512], F32))
            fence = p.fence_all()

            win_dep = [None, None]
            f2x_dep = [None, None]
            f2st_dep = [None, None]
            a3_busy = [None] * 64
            wwin_free = None
            silu2_dep = None
            out_last = None
            for half in range(2):
                for w in range(16):
                    buf = w % 2
                    offg = buf * 16384
                    offu = buf * 16384 + 8192
                    gparts = []
                    uparts = []
                    for k in range(NK):
                        dgp = p.dma("scalar",
                                    wwin[:, offg + k * 512: offg + (k + 1) * 512],
                                    wgate_bf[ts(w, 512), ts(k, 128)],
                                    waits=[win_dep[buf], wwin_free] + d_gbf + fence,
                                    transpose=True)
                        gparts.append(dgp)
                        dup = p.dma("scalar",
                                    wwin[:, offu + k * 512: offu + (k + 1) * 512],
                                    wup_bf[ts(w, 512), ts(k, 128)],
                                    waits=d_ubf, transpose=True)
                        uparts.append(dup)
                    last = None
                    for hh in range(4):
                        hid = w * 4 + hh
                        bg = pb[hh % 2]
                        bu = pb[2 + hh % 2]
                        p.waits("tensor",
                                gparts + [bank_free[hh % 2], a3_busy[hid]] + tr_xb)
                        for k in range(NK):
                            def f(E, k=k, half=half, bg=bg, offg=offg, hh=hh):
                                return E.matmul(
                                    bg[:, 0:512],
                                    wwin[:, offg + k * 512 + hh * 128: offg + k * 512 + (hh + 1) * 128],
                                    arena[:, k * T + half * 512: k * T + (half + 1) * 512],
                                    start=(k == 0), stop=(k == NK - 1))
                            dg_ = p.op("tensor", f, inc=(k == NK - 1))
                        p.waits("tensor", uparts + [bank_free[2 + hh % 2]])
                        for k in range(NK):
                            def f(E, k=k, half=half, bu=bu, offu=offu, hh=hh):
                                return E.matmul(
                                    bu[:, 0:512],
                                    wwin[:, offu + k * 512 + hh * 128: offu + k * 512 + (hh + 1) * 128],
                                    arena[:, k * T + half * 512: k * T + (half + 1) * 512],
                                    start=(k == 0), stop=(k == NK - 1))
                            du_ = p.op("tensor", f, inc=(k == NK - 1))
                        wwin_last_mm = du_
                        dsl = p.op("scalar", lambda E, bg=bg: E.activation(
                            silu2[:, 0:512], bg[:, 0:512], AF.Silu),
                            waits=[dg_, silu2_dep] + fence)
                        bank_free[hh % 2] = dsl
                        da3 = p.op("vector", lambda E, hid=hid, bu=bu: E.tensor_mul(
                            a3[:, ts(hid, 512)], silu2[:, 0:512], bu[:, 0:512]),
                            waits=[dsl, du_] + fence)
                        silu2_dep = da3
                        bank_free[2 + hh % 2] = da3
                        a3_busy[hid] = da3
                        last = da3
                    win_dep[buf] = last
                    wwin_free = wwin_last_mm
                for s in range(4):
                    dwd0 = wload("gpsimd", wwin[:, 0:16384],
                                 wdownT[0:4096, ts(s, 512)], 512,
                                 waits=[win_dep[0], win_dep[1], dT_last, wwin_free])
                    dwd1 = wload("gpsimd", wwin[:, 16384:32768],
                                 wdownT[4096:8192, ts(s, 512)], 512)
                    win_dep[0] = None
                    win_dep[1] = None
                    for tt in range(4):
                        t = half * 4 + tt
                        bk = pb[4 + tt % 2]
                        p.waits("tensor",
                                [dwd0, dwd1, bank_free[4 + tt % 2], a3_busy[63]])
                        for hid in range(64):
                            def f(E, hid=hid, tt=tt, bk=bk):
                                return E.matmul(
                                    bk[:, 0:512],
                                    a3[:, hid * 512 + tt * 128: hid * 512 + (tt + 1) * 128],
                                    wwin[:, hid * 512: (hid + 1) * 512],
                                    start=(hid == 0), stop=(hid == 63))
                            dmm = p.op("tensor", f, inc=(hid == 63))
                        wwin_free = dmm
                        xb = tt % 2
                        dxB = p.dma("sync", f2x[:, ts(xb, 512)],
                                    xBd[ts(t, 128), ts(s, 512)],
                                    waits=[f2x_dep[xb]] + fence)
                        d0 = p.op("vector", lambda E, xb=xb, bk=bk: E.tensor_add(
                            f2st[:, ts(xb, 512)], f2x[:, ts(xb, 512)],
                            bk[:, 0:512]),
                            waits=[dxB, dmm, f2st_dep[xb]] + fence)
                        bank_free[4 + tt % 2] = d0
                        f2x_dep[xb] = d0
                        dst_ = p.dma("sync", out[ts(t, 128), ts(s, 512)],
                                     f2st[:, ts(xb, 512)], waits=[d0])
                        f2st_dep[xb] = dst_
                        out_last = dst_
                for i in range(64):
                    a3_busy[i] = out_last

            p.waits("sync", [out_last] +
                    [(f"ss{l}", p.cnt[f"ss{l}"]) for l in range(DMA_LANES)])

        with nc.Block() as blk:
            @blk.sync
            def _(E):
                for f in p.ops["sync"]:
                    f(E)

            @blk.scalar
            def _(E):
                for f in p.ops["scalar"]:
                    f(E)

            @blk.vector
            def _(E):
                for f in p.ops["vector"]:
                    f(E)

            @blk.tensor
            def _(E):
                for f in p.ops["tensor"]:
                    f(E)

            @blk.gpsimd
            def _(E):
                for f in p.ops["gpsimd"]:
                    f(E)

    return nc


def _host_tables():
    gam = 1.0 - 2.0 ** (-5.0 - np.arange(H, dtype=np.float64))
    f_ = np.arange(512)
    p_ = np.arange(128)
    decs = np.zeros((H, 8, 128, 512), np.float32)
    for h in range(H):
        for r in range(8):
            d_ = (r - 3) * 128 + f_[None, :] - p_[:, None]
            decs[h, r] = np.where(d_ >= 0, gam[h] ** np.maximum(d_, 0), 0.0
                                  ).astype(np.float32)
    a_ = np.arange(T, dtype=np.float64)
    gq = np.concatenate([(g ** (a_ + 1.0)) for g in gam]).astype(ml_dtypes.bfloat16)
    gq = np.ascontiguousarray(gq.reshape(1, H * T))
    return decs, gq, gam


def _kgc_for(loc, gam):
    kgc = np.zeros((128, NPRE * NT * H), np.float32)
    p_ = np.arange(128, dtype=np.float64)
    for c in range(NPRE):
        if c >= loc:
            continue
        for t in range(NT):
            pos = t * 128 + p_
            for h in range(H):
                expo = (1023.0 - pos) + 1024.0 * (loc - 1 - c)
                kgc[:, (c * NT + t) * H + h] = (gam[h] ** expo) * SCL
    return kgc


def kernel(L0_attn_x, L0_attn_norm1_w, L0_attn_norm1_b, L0_attn_wq, L0_attn_wk,
           L0_attn_wv, L0_attn_wg, L0_attn_wo, L0_attn_norm2_w, L0_attn_norm2_b,
           L0_attn_w1, L0_attn_w2, L0_attn_w3, L0_ffn_w_gate, L0_ffn_w_up,
           L0_ffn_w_down):
    xfull = np.ascontiguousarray(np.asarray(L0_attn_x, np.float32))
    decs, gq, gam = _host_tables()
    f32 = lambda a: np.ascontiguousarray(np.asarray(a, np.float32))
    shared = dict(
        wq=f32(L0_attn_wq), wk=f32(L0_attn_wk), wv=f32(L0_attn_wv),
        wg=f32(L0_attn_wg), wo=f32(L0_attn_wo),
        w1=f32(L0_attn_w1), w2=f32(L0_attn_w2), w3=f32(L0_attn_w3),
        wgate=f32(L0_ffn_w_gate), wup=f32(L0_ffn_w_up), wdown=f32(L0_ffn_w_down),
        dec=decs, gqrow=gq,
        n1w=f32(L0_attn_norm1_w).reshape(1, D),
        n1b=f32(L0_attn_norm1_b).reshape(1, D),
        n2w=f32(L0_attn_norm2_w).reshape(1, D),
        n2b=f32(L0_attn_norm2_b).reshape(1, D),
    )
    in_maps = []
    for c in range(8):
        seq, loc = c // 4, c % 4
        m = dict(shared)
        m["x"] = np.ascontiguousarray(xfull[seq, loc * T:(loc + 1) * T, :])
        m["xpre"] = np.ascontiguousarray(xfull[seq, 0:NPRE * T, :])
        m["kgc"] = _kgc_for(loc, gam)
        in_maps.append(m)

    nc = build_program()
    res = run_bass_kernel_spmd(nc, in_maps, core_ids=list(range(8)))
    outs = [np.asarray(res.results[c]["out"], np.float32) for c in range(8)]
    return np.stack([np.concatenate(outs[0:4], 0), np.concatenate(outs[4:8], 0)], 0)


if __name__ == "__main__":
    nc = build_program()
    n = sum(1 for _ in nc.all_instructions())
    print("instructions:", n)


# version 06 (final)
# speedup vs baseline: 1.1057x; 1.0325x over previous
import numpy as np
import ml_dtypes
from contextlib import ExitStack
import concourse.bass as bass
import concourse.mybir as mybir
from concourse.bass_utils import run_bass_kernel_spmd

F32 = mybir.dt.float32
BF16 = mybir.dt.bfloat16
AF = mybir.ActivationFunctionType
ALU = mybir.AluOpType
AX = mybir.AxisListType.X

B, S, D = 2, 4096, 2048
H, DH = 16, 128
HID1, HID2 = 5461, 8192
T = 1024
NT = T // 128
NK = D // 128
NPRE = 3
EPS = 1e-5
SCL = float(DH) ** -0.5

ENG = ("sync", "scalar", "vector", "tensor", "gpsimd")


def ts(i, n):
    return slice(i * n, (i + 1) * n)


DMA_LANES = 4


class P:
    def __init__(self, nc):
        self.nc = nc
        self.ops = {e: [] for e in ENG}
        self.sem = {}
        self.cnt = {}
        self.hw = {e: {} for e in ENG}
        self.dma_i = {"ss": 0, "sa": 0, "sq": 0}

    def add_sem(self, name, handle):
        self.sem[name] = handle
        self.cnt[name] = 0

    def wait(self, eng, dep):
        if dep is None:
            return
        s, v = dep
        if v <= 0:
            return
        if self.hw[eng].get(s, 0) >= v:
            return
        self.hw[eng][s] = v
        h = self.sem[s]
        self.ops[eng].append(lambda E, h=h, v=v: E.wait_ge(h, v))

    def waits(self, eng, deps):
        for d in deps:
            self.wait(eng, d)

    def fence_all(self):
        return [(s, self.cnt[s]) for s in self.sem]

    def dma(self, eng, out, in_, waits=(), transpose=False):
        base = {"sync": "ss", "scalar": "sa", "gpsimd": "sq"}[eng]
        i = self.dma_i[base]
        self.dma_i[base] = i + 1
        lane = f"{base}{i % DMA_LANES}"
        self.wait(eng, (lane, 16 * (i // DMA_LANES)))
        self.waits(eng, waits)
        v = self.cnt[lane] = 16 * (i // DMA_LANES + 1)
        h = self.sem[lane]

        def f(E, out=out, in_=in_, h=h, transpose=transpose):
            E.dma_start(out=out, in_=in_, transpose=transpose).then_inc(h, 16)

        self.ops[eng].append(f)
        return (lane, v)

    def op(self, eng, fn, waits=(), inc=True):
        sname = {"tensor": "pt", "vector": "pv", "scalar": "pc", "gpsimd": "pg"}[eng]
        if eng != "tensor":
            inc = True
            self.wait(eng, (sname, self.cnt[sname]))
        self.waits(eng, waits)
        if inc:
            v = self.cnt[sname] = self.cnt[sname] + 1
            h = self.sem[sname]

            def f(E, fn=fn, h=h):
                fn(E).then_inc(h, 1)

            self.ops[eng].append(f)
            return (sname, v)
        self.ops[eng].append(lambda E, fn=fn: fn(E))
        return None


def build_program():
    nc = bass.Bass("TRN2", target_bir_lowering=False)

    di = lambda n, sh, dt=F32: nc.dram_tensor(n, sh, dt, kind="ExternalInput").ap()
    x = di("x", [T, D])
    wq, wk, wv, wg, wo = (di(n, [D, D]) for n in ("wq", "wk", "wv", "wg", "wo"))
    w1, w2 = di("w1", [D, HID1]), di("w2", [D, HID1])
    w3 = di("w3", [HID1, D])
    wgate, wup = di("wgate", [HID2, D]), di("wup", [HID2, D])
    wdown = di("wdown", [D, HID2])
    dec = di("dec", [H, 8, 128, 512])
    gqrow = di("gqrow", [1, H * T], BF16)
    kg0 = di("kg0", [128, NT * H])
    coef = di("coef", [128, 8 * H])
    n1w, n1b = di("n1w", [1, D]), di("n1b", [1, D])
    n2w, n2b = di("n2w", [1, D]), di("n2b", [1, D])
    out = nc.dram_tensor("out", [T, D], F32, kind="ExternalOutput").ap()

    dd = lambda n, sh, dt: nc.dram_tensor(n, sh, dt).ap()
    s1 = dd("s1", [T, D], BF16)
    x2d = dd("x2d", [T, D], F32)
    xBd = dd("xBd", [T, D], F32)
    wgate_bf = dd("wgate_bf", [HID2, D], BF16)
    wup_bf = dd("wup_bf", [HID2, D], BF16)
    wdown_bf = dd("wdown_bf", [D, HID2], BF16)
    wdownT = dd("wdownT", [HID2, D], BF16)
    r_own_dt = nc.dram_tensor("r_own_d", [128, H * 128], BF16)
    r_all_dt = nc.dram_tensor("r_all_d", [8 * 128, H * 128], BF16)
    r_own_d = r_own_dt.ap()
    r_all_d = r_all_dt.ap()

    with ExitStack() as st:
        e = st.enter_context
        p = P(nc)
        for base in ("ss", "sa", "sq"):
            for l in range(DMA_LANES):
                p.add_sem(f"{base}{l}", e(nc.semaphore(f"{base}{l}")))
        for sname in ("pt", "pv", "pc", "pg"):
            p.add_sem(sname, e(nc.semaphore(sname)))

        arena = e(nc.sbuf_tensor("arena", [128, NK * T], BF16, is_transpose_dest=True))
        stats = e(nc.sbuf_tensor("stats", [128, 64], F32))
        ones1 = e(nc.sbuf_tensor("ones1", [1, 128], BF16))
        onesS = e(nc.sbuf_tensor("onesS", [128, 128], F32))
        gq_s = e(nc.sbuf_tensor("gq_s", [1, T], BF16))
        epsb = e(nc.sbuf_tensor("epsb", [128, 1], F32))
        eps2b = e(nc.sbuf_tensor("eps2b", [128, 1], F32))
        pb = [e(nc.psum_tensor(f"pb{i}", [128, 512], F32)) for i in range(8)]
        bank_free = [None] * 8

        p.op("vector", lambda E: E.memset(ones1[:, :], 1.0), inc=False)
        p.op("vector", lambda E: E.memset(epsb[:, :], EPS), inc=False)
        p.op("vector", lambda E: E.memset(eps2b[:, :], EPS * EPS), inc=False)
        d_ones = p.op("vector", lambda E: E.memset(onesS[:, :], 1.0))

        d_gbf = [p.dma("gpsimd", wgate_bf[ts(i, 2048), :], wgate[ts(i, 2048), :])
                 for i in range(4)]
        d_ubf = [p.dma("gpsimd", wup_bf[ts(i, 2048), :], wup[ts(i, 2048), :])
                 for i in range(4)]
        d_dbf = [p.dma("gpsimd", wdown_bf[ts(i, 512), :], wdown[ts(i, 512), :])
                 for i in range(4)]

        s1_free = [[]]

        def wload(eng, dst_ap, src_ap, n, waits=()):
            return p.dma(eng, dst_ap.rearrange("p (a n) -> p a n", n=n),
                         src_ap.rearrange("(a p) n -> p a n", p=128), waits=waits)

        def replicate_rows(dst, dep):
            deps = []
            for j in range(4):
                dm = p.op("tensor", lambda E, j=j, dst=dst: E.matmul(
                    pb[7][:, 0:512], onesS[0:1, :], dst[0:1, ts(j, 512)],
                    start=True, stop=True), waits=[dep, d_ones, bank_free[7]])
                dc = p.op("vector", lambda E, j=j, dst=dst: E.tensor_copy(
                    dst[:, ts(j, 512)], pb[7][:, 0:512]), waits=[dm])
                bank_free[7] = dc
                deps.append(dc)
            return deps

        _uid = [0]

        def ln_block(src_ap, dst_arena, affine, col0):
            fence = p.fence_all()
            _uid[0] += 1
            u = _uid[0]
            with ExitStack() as ls:
                le = ls.enter_context
                xt = le(nc.sbuf_tensor(f"ln_xt{u}", [128, 2 * 2048], F32))
                sq = le(nc.sbuf_tensor(f"ln_sq{u}", [128, 2048], F32))
                tst = le(nc.sbuf_tensor(f"ln_tst{u}", [128, 16], F32))
                ht = le(nc.sbuf_tensor(f"ln_ht{u}", [128, 2 * 2048], BF16))
                nrm = sq
                if affine is not None:
                    wrep = le(nc.sbuf_tensor(f"ln_wrep{u}", [128, 2048], F32))
                    brep = le(nc.sbuf_tensor(f"ln_brep{u}", [128, 2048], F32))
                    dw_ = p.dma("sync", wrep[0:1, :], affine[1], waits=fence)
                    db_ = p.dma("sync", brep[0:1, :], affine[2], waits=fence)
                    rep = replicate_rows(wrep, dw_) + replicate_rows(brep, db_)
                else:
                    rep = []
                st_deps = []
                prev_cons = [None, None]
                prev_stage = [None, None]
                sq_free = None
                for t in range(NT):
                    sl = ts(t % 2, 2048)
                    dld = p.dma("sync", xt[:, sl], src_ap[ts(t, 128), :],
                                waits=[prev_cons[t % 2]] + rep + fence)
                    dsum = p.op("vector", lambda E, sl=sl: E.reduce_sum(
                        tst[:, 0:1], xt[:, sl], axis=AX), waits=[dld] + fence)
                    dsq = p.op("scalar", lambda E, sl=sl: E.activation(
                        sq[:, :], xt[:, sl], AF.Square), waits=[dld, sq_free] + fence)
                    dsum2 = p.op("vector", lambda E: E.reduce_sum(
                        tst[:, 1:2], sq[:, :], axis=AX), waits=[dsq])
                    sq_free = dsum2
                    cn, cr = col0 + 2 * t, col0 + 2 * t + 1
                    p.op("vector", lambda E, cn=cn: E.tensor_scalar_mul(
                        stats[:, cn:cn + 1], tst[:, 0:1], -1.0 / D),
                        waits=[dsum], inc=False)
                    p.op("vector", lambda E: E.tensor_scalar_mul(
                        tst[:, 2:3], tst[:, 1:2], 1.0 / D), waits=[dsum2], inc=False)
                    p.op("vector", lambda E, cn=cn: E.tensor_mul(
                        tst[:, 3:4], stats[:, cn:cn + 1], stats[:, cn:cn + 1]),
                        inc=False)
                    dvar = p.op("vector", lambda E: E.tensor_sub(
                        tst[:, 4:5], tst[:, 2:3], tst[:, 3:4]))
                    dln_ = p.op("scalar", lambda E: E.activation(
                        tst[:, 5:6], tst[:, 4:5], AF.Ln, bias=epsb[:, 0:1]), waits=[dvar])
                    drs = p.op("scalar", lambda E, cr=cr: E.activation(
                        stats[:, cr:cr + 1], tst[:, 5:6], AF.Exp, scale=-0.5),
                        waits=[dln_])
                    if affine is not None and affine[0] == "combined":
                        cc1 = col0 + 16 + t
                        dl2 = p.op("scalar", lambda E: E.activation(
                            tst[:, 9:10], tst[:, 4:5], AF.Ln,
                            bias=eps2b[:, 0:1], scale=1.0 + EPS), waits=[dvar])
                        dsc = p.op("scalar", lambda E, cc1=cc1: E.activation(
                            stats[:, cc1:cc1 + 1], tst[:, 9:10], AF.Exp,
                            scale=-0.5), waits=[dl2])
                        scol = cc1
                    else:
                        dsc, scol = drs, cr
                    hsl = ts(t % 2, 2048)
                    if affine is None:
                        dn = p.op("vector", lambda E, sl=sl, hsl=hsl, cn=cn, scol=scol:
                                  E.tensor_scalar(ht[:, hsl], xt[:, sl],
                                                  stats[:, cn:cn + 1],
                                                  stats[:, scol:scol + 1],
                                                  ALU.add, ALU.mult),
                                  waits=[dsc, prev_stage[t % 2]])
                    else:
                        d0 = p.op("vector", lambda E, sl=sl, cn=cn, scol=scol:
                                  E.tensor_scalar(nrm[:, :], xt[:, sl],
                                                  stats[:, cn:cn + 1],
                                                  stats[:, scol:scol + 1],
                                                  ALU.add, ALU.mult),
                                  waits=[dsc])
                        d1 = p.op("vector", lambda E: E.tensor_mul(
                            nrm[:, :], nrm[:, :], wrep[:, :]), waits=[d0])
                        dn = p.op("vector", lambda E, hsl=hsl: E.tensor_add(
                            ht[:, hsl], nrm[:, :], brep[:, :]),
                            waits=[d1, prev_stage[t % 2]])
                    prev_cons[t % 2] = dn
                    sq_free = dn
                    dst_ = p.dma("sync", s1[ts(t, 128), :], ht[:, hsl],
                                 waits=[dn] + s1_free[0])
                    prev_stage[t % 2] = dst_
                    st_deps.append(dst_)
                tr_deps = []
                pe_now = ("pt", p.cnt["pt"])
                for k in range(NK):
                    dtr = p.dma("scalar", dst_arena[:, ts(k, T)], s1[:, ts(k, 128)],
                                waits=st_deps + [pe_now], transpose=True)
                    tr_deps.append(dtr)
                s1_free[0] = tr_deps
                return tr_deps

        with ExitStack() as hkv_st:
            he = hkv_st.enter_context
            h_f = he(nc.sbuf_tensor("h_f", [128, NK * T], BF16, is_transpose_dest=True))
            r_bf = he(nc.sbuf_tensor("r_bf", [128, H * 128], BF16))

            tr_h = ln_block(x, h_f, ("combined", n1w, n1b), 0)


            v_T = he(nc.sbuf_tensor("v_T", [128, NT * D], BF16))
            m_st = ExitStack()
            wslab = m_st.enter_context(
                nc.sbuf_tensor("wslab_m", [128, 2 * 8192], BF16))
            fence = p.fence_all()
            wv_dep = [None, None]
            for s in range(4):
                buf = s % 2
                vo = buf * 8192
                wv_dep[buf] = wload(
                    "gpsimd", wslab[:, vo:vo + 8192],
                    wv[:, ts(s, 512)], 512,
                    waits=[wv_dep[buf]] + fence)
                ev = None
                for t in range(NT):
                    bv = pb[t % 4]
                    p.waits("tensor", [wv_dep[buf], bank_free[t % 4]] + tr_h)
                    for k in range(NK):
                        def f(E, k=k, t=t, bv=bv, vo=vo):
                            return E.matmul(
                                bv[:, 0:512],
                                h_f[:, k * T + t * 128: k * T + t * 128 + 128],
                                wslab[:, vo + k * 512: vo + (k + 1) * 512],
                                start=(k == 0), stop=(k == NK - 1))
                        dmv = p.op("tensor", f, inc=(k == NK - 1))
                    ev = p.op("vector", lambda E, s=s, t=t, bv=bv: E.tensor_copy(
                        v_T[:, t * 2048 + s * 512: t * 2048 + (s + 1) * 512],
                        bv[:, 0:512]), waits=[dmv])
                    bank_free[t % 4] = ev
                wv_dep[buf] = ev

            with ExitStack() as so_st:
                se = so_st.enter_context
                kw_q = se(nc.sbuf_tensor("kw_q", [128, NT * 512], BF16))
                r_own = se(nc.sbuf_tensor("r_own", [128, H * 128], BF16))
                rb_s = se(nc.sbuf_tensor("rb_s", [128, H * 128], BF16))
                tmp_s = se(nc.sbuf_tensor("tmp_s", [128, H * 128], F32))
                r_acc = se(nc.sbuf_tensor("r_acc", [128, H * 128], F32))
                kg0_s = se(nc.sbuf_tensor("kg0_s", [128, NT * H], F32))
                coef_s = se(nc.sbuf_tensor("coef_s", [128, 8 * H], F32))
                fence_s = p.fence_all()
                d_kg0 = p.dma("sync", kg0_s[:, :], kg0, waits=fence_s)
                d_coef = p.dma("sync", coef_s[:, :], coef, waits=fence_s)
                kw_rd = [None, None]
                dro_last = None
                for s4 in range(4):
                    buf = s4 % 2
                    ko = buf * 8192
                    wkd = wload("gpsimd", wslab[:, ko:ko + 8192],
                                wk[:, ts(s4, 512)], 512,
                                waits=[wv_dep[buf], kw_rd[buf]] + fence_s)
                    ev = None
                    for t in range(NT):
                        bk = pb[t % 4]
                        p.waits("tensor", [wkd, bank_free[t % 4]] + tr_h)
                        for k in range(NK):
                            def f(E, k=k, t=t, bk=bk, ko=ko):
                                return E.matmul(
                                    bk[:, 0:512],
                                    h_f[:, k * T + t * 128: k * T + t * 128 + 128],
                                    wslab[:, ko + k * 512: ko + (k + 1) * 512],
                                    start=(k == 0), stop=(k == NK - 1))
                            dmm = p.op("tensor", f, inc=(k == NK - 1))
                        for hh in range(4):
                            head = 4 * s4 + hh
                            kc = t * H + head
                            ev = p.op(
                                "vector",
                                lambda E, t=t, hh=hh, kc=kc, bk=bk:
                                E.tensor_scalar_mul(
                                    kw_q[:, t * 512 + hh * 128: t * 512 + (hh + 1) * 128],
                                    bk[:, hh * 128:(hh + 1) * 128],
                                    kg0_s[:, kc:kc + 1]),
                                waits=[dmm, d_kg0, kw_rd[buf]] + fence_s)
                        bank_free[t % 4] = ev
                    bq = pb[4 + s4]
                    p.waits("tensor", [ev, bank_free[4 + s4]])
                    for hh in range(4):
                        head = 4 * s4 + hh
                        for t in range(NT):
                            def f(E, t=t, hh=hh, head=head, bq=bq):
                                return E.matmul(
                                    bq[:, hh * 128:(hh + 1) * 128],
                                    kw_q[:, t * 512 + hh * 128: t * 512 + (hh + 1) * 128],
                                    v_T[:, t * D + head * 128: t * D + (head + 1) * 128],
                                    start=(t == 0), stop=(t == NT - 1))
                            dmr = p.op("tensor", f, inc=(t == NT - 1))
                    kw_rd[buf] = dmr
                    dro = p.op("vector", lambda E, s4=s4, bq=bq: E.tensor_copy(
                        r_own[:, ts(s4, 512)], bq[:, 0:512]), waits=[dmr])
                    bank_free[4 + s4] = dro
                    dro_last = dro
                d_st = p.dma("gpsimd", r_own_d[:, :], r_own[:, :],
                             waits=[dro_last])
                d_cc = p.op("gpsimd", lambda E: E.collective_compute(
                    "AllGather", ALU.bypass,
                    replica_groups=[list(range(8))],
                    ins=[r_own_dt.ap().opt()],
                    outs=[r_all_dt.ap().opt()]), waits=[d_st])
                d_r0 = p.op("vector", lambda E: E.memset(r_acc[:, :], 0.0))
                comb = None
                for r in range(8):
                    drb = p.dma("sync", rb_s[:, :], r_all_d[ts(r, 128), :],
                                waits=[d_cc, comb] + fence_s)
                    dmu = None
                    for head in range(H):
                        cc = r * H + head
                        dmu = p.op("vector",
                                   lambda E, head=head, cc=cc:
                                   E.tensor_scalar_mul(
                                       tmp_s[:, ts(head, 128)],
                                       rb_s[:, ts(head, 128)],
                                       coef_s[:, cc:cc + 1]),
                                   waits=[drb, d_coef])
                    comb = p.op("vector", lambda E: E.tensor_add(
                        r_acc[:, :], r_acc[:, :], tmp_s[:, :]),
                        waits=[dmu, d_r0])
                d_rbf = p.op("vector", lambda E: E.tensor_copy(
                    r_bf[:, :], r_acc[:, :]))
            m_st.close()
            k_f = he(nc.sbuf_tensor("k_f", [128, H * T], BF16))
            wcb = he(nc.sbuf_tensor("wcb", [128, 2 * 2048], BF16))
            wcb_free = [None, None]
            fence_m = p.fence_all()

            def feat_proj(wdram, head, evac_cb, banks=(0, 1)):
                buf = head % 2
                off = buf * 2048
                wdep = wload("gpsimd", wcb[:, off:off + 2048],
                             wdram[:, ts(head, 128)], 128,
                             waits=[wcb_free[buf]] + fence_m)
                outs = []
                last_mm = None
                for half in range(2):
                    bk = pb[banks[half]]
                    p.waits("tensor", [wdep, bank_free[banks[half]]] + tr_h)
                    for k in range(NK):
                        def f(E, k=k, half=half, bk=bk, off=off):
                            return E.matmul(
                                bk[:, 0:512],
                                wcb[:, off + k * 128: off + (k + 1) * 128],
                                h_f[:, k * T + half * 512: k * T + (half + 1) * 512],
                                start=(k == 0), stop=(k == NK - 1))
                        dmm = p.op("tensor", f, inc=(k == NK - 1))
                    last_mm = dmm
                    ev = evac_cb(half, bk, dmm)
                    bank_free[banks[half]] = ev
                    outs.append(ev)
                wcb_free[buf] = last_mm
                return outs

            for head in range(H):
                def evk(half, bk, gdep, head=head):
                    return p.op("vector", lambda E, half=half, bk=bk, head=head:
                                E.tensor_scalar_mul(
                                    k_f[:, head * T + half * 512: head * T + (half + 1) * 512],
                                    bk[:, 0:512], SCL), waits=[gdep])
                feat_proj(wk, head, evk)

            with ExitStack() as rs:
                re_ = rs.enter_context
                qb = re_(nc.sbuf_tensor("qb", [128, T], BF16))
                qg = re_(nc.sbuf_tensor("qg", [128, T], BF16))
                sbf = re_(nc.sbuf_tensor("sbf", [128, 12 * 512], BF16))
                decb = re_(nc.sbuf_tensor("decb", [128, 3 * 512], F32))
                retsb = re_(nc.sbuf_tensor("retsb", [128, T], F32))
                gnt = re_(nc.sbuf_tensor("gnt", [128, T], F32))
                negmb = re_(nc.sbuf_tensor("negmb", [128, T], F32))
                rstdb = re_(nc.sbuf_tensor("rstdb", [128, T], F32))
                gsb = re_(nc.sbuf_tensor("gsb", [128, T], F32))
                fence = p.fence_all()

                pairs = [(0, 0), (0, 1), (0, 2), (0, 3),
                         (1, 0), (1, 1), (1, 2), (1, 3),
                         (1, 4), (1, 5), (1, 6), (1, 7)]
                dec_slot = [None, None, None]
                q_cons = None
                g_cons = None
                gq_cons = None
                head_done = None
                sbf_free = None
                for head in range(H):
                    def evq(half, bk, gdep):
                        return p.op("vector", lambda E, half=half, bk=bk:
                                    E.tensor_copy(qb[:, ts(half, 512)],
                                                  bk[:, 0:512]),
                                    waits=[gdep, q_cons] + fence)
                    qouts = feat_proj(wq, head, evq)

                    dgq = p.dma("sync", gq_s[0:1, :], gqrow[0:1, ts(head, T)],
                                waits=[gq_cons])
                    qg_deps = []
                    for half in range(2):
                        p.waits("tensor", [dgq, qouts[half], bank_free[6]])
                        dbc = p.op("tensor", lambda E, half=half: E.matmul(
                            pb[6][:, 0:512], ones1[0:1, :],
                            gq_s[0:1, ts(half, 512)], start=True, stop=True))
                        dqg = p.op("vector", lambda E, half=half: E.tensor_mul(
                            qg[:, ts(half, 512)], qb[:, ts(half, 512)],
                            pb[6][:, 0:512]), waits=[dbc])
                        bank_free[6] = dqg
                        qg_deps.append(dqg)
                    gq_cons = qg_deps[-1]

                    sc_deps = {}
                    for i, (qt, kt) in enumerate(pairs):
                        r = 4 * qt - kt + 3
                        ddec = p.dma(
                            "scalar", decb[:, ts(i % 3, 512)], dec[head, r, :, :],
                            waits=[dec_slot[i % 3]] + fence)
                        bk = pb[2 + (i % 2)]
                        p.waits("tensor", [qouts[qt], bank_free[2 + (i % 2)]])
                        dsc = p.op("tensor", lambda E, qt=qt, kt=kt, bk=bk, head=head:
                                   E.matmul(
                                       bk[:, 0:512],
                                       k_f[:, head * T + kt * 128: head * T + kt * 128 + 128],
                                       qb[:, ts(qt, 512)], start=True, stop=True))
                        slot = kt if qt == 0 else 4 + kt
                        dsb = p.op("vector", lambda E, slot=slot, i=i, bk=bk:
                                   E.tensor_mul(sbf[:, ts(slot, 512)], bk[:, 0:512],
                                                decb[:, ts(i % 3, 512)]),
                                   waits=[dsc, ddec, sbf_free])
                        dec_slot[i % 3] = dsb
                        bank_free[2 + (i % 2)] = dsb
                        sc_deps[(qt, kt)] = dsb

                    ret_deps = []
                    last_ret_mm = None
                    for qt in range(2):
                        bk = pb[4 + qt]
                        nkt = 4 if qt == 0 else 8
                        p.waits("tensor", [qg_deps[qt], d_rbf, bank_free[4 + qt]])
                        p.op("tensor", lambda E, qt=qt, bk=bk, head=head: E.matmul(
                            bk[:, 0:512], r_bf[:, ts(head, 128)],
                            qg[:, ts(qt, 512)], start=True, stop=False), inc=False)
                        for kt in range(nkt):
                            slot = kt if qt == 0 else 4 + kt
                            p.waits("tensor", [sc_deps[(qt, kt)]])

                            def f(E, qt=qt, kt=kt, slot=slot, bk=bk, head=head):
                                return E.matmul(
                                    bk[:, 0:512],
                                    v_T[:, kt * 2048 + head * 128: kt * 2048 + (head + 1) * 128],
                                    sbf[:, ts(slot, 512)],
                                    start=False, stop=(kt == nkt - 1))
                            dmm = p.op("tensor", f, inc=(kt == nkt - 1))
                        last_ret_mm = dmm
                        dev = p.op("vector", lambda E, qt=qt, bk=bk: E.tensor_copy(
                            retsb[:, ts(qt, 512)], bk[:, 0:512]),
                            waits=[dmm, head_done])
                        bank_free[4 + qt] = dev
                        ret_deps.append(dev)
                    sbf_free = last_ret_mm

                    dsq = p.op("scalar", lambda E: E.activation(
                        gnt[:, :], retsb[:, :], AF.Square),
                        waits=ret_deps + [head_done] + fence)
                    stat_deps = []
                    for half in range(2):
                        p.waits("tensor", ret_deps + [bank_free[6]])
                        dm = p.op("tensor", lambda E, half=half: E.matmul(
                            pb[6][:, 0:512], onesS[:, :], retsb[:, ts(half, 512)],
                            start=True, stop=True))
                        p.waits("tensor", [dsq, bank_free[7]])
                        ds2 = p.op("tensor", lambda E, half=half: E.matmul(
                            pb[7][:, 0:512], onesS[:, :], gnt[:, ts(half, 512)],
                            start=True, stop=True))
                        dnm = p.op("vector", lambda E, half=half:
                                   E.tensor_scalar_mul(negmb[:, ts(half, 512)],
                                                       pb[6][:, 0:512], -1.0 / 128),
                                   waits=[dm])
                        bank_free[6] = dnm
                        dm2 = p.op("vector", lambda E, half=half: E.tensor_mul(
                            gnt[:, ts(half, 512)], negmb[:, ts(half, 512)],
                            negmb[:, ts(half, 512)]), waits=[dnm, dsq])
                        dvr = p.op("vector", lambda E, half=half:
                                   E.scalar_tensor_tensor(
                                       gnt[:, ts(half, 512)], pb[7][:, 0:512],
                                       1.0 / 128, gnt[:, ts(half, 512)],
                                       ALU.mult, ALU.subtract),
                                   waits=[ds2, dm2])
                        bank_free[7] = dvr
                        dl = p.op("scalar", lambda E, half=half: E.activation(
                            rstdb[:, ts(half, 512)], gnt[:, ts(half, 512)],
                            AF.Ln, bias=epsb[:, 0:1]), waits=[dvr])
                        dr = p.op("scalar", lambda E, half=half: E.activation(
                            rstdb[:, ts(half, 512)], rstdb[:, ts(half, 512)],
                            AF.Exp, scale=-0.5), waits=[dl])
                        stat_deps.append(dr)

                    def evg(half, bk, gdep):
                        return p.op("scalar", lambda E, half=half, bk=bk:
                                    E.activation(gsb[:, ts(half, 512)],
                                                 bk[:, 0:512], AF.Silu),
                                    waits=[gdep, g_cons] + fence)
                    gouts = feat_proj(wg, head, evg, banks=(2, 3))
                    g_cons = gouts[-1]

                    a_deps = []
                    for half in range(2):
                        d1 = p.op("vector", lambda E, half=half: E.tensor_add(
                            retsb[:, ts(half, 512)], retsb[:, ts(half, 512)],
                            negmb[:, ts(half, 512)]), waits=stat_deps + ret_deps)
                        d2 = p.op("vector", lambda E, half=half: E.tensor_mul(
                            retsb[:, ts(half, 512)], retsb[:, ts(half, 512)],
                            rstdb[:, ts(half, 512)]), waits=[d1])
                        d3 = p.op("vector", lambda E, half=half, head=head:
                                  E.tensor_mul(
                                      arena[:, head * T + half * 512: head * T + (half + 1) * 512],
                                      retsb[:, ts(half, 512)],
                                      gsb[:, ts(half, 512)]),
                                  waits=[d2, gouts[half]])
                        a_deps.append(d3)
                    q_cons = a_deps[-1]
                    head_done = a_deps[-1]
                a_done = head_done

        with ExitStack() as ys:
            ye = ys.enter_context
            woq = ye(nc.sbuf_tensor("woq", [128, 2 * 8192], BF16))
            yx = ye(nc.sbuf_tensor("yx", [128, 2 * 512], F32))
            yst = ye(nc.sbuf_tensor("yst", [128, 2 * 512], F32))
            tstg = ye(nc.sbuf_tensor("tstg", [128, 4 * 2048], BF16,
                                     is_transpose_dest=True))
            fence = p.fence_all()

            wo_dep = [None, None]
            yx_dep = [None, None]
            yst_dep = [None, None]
            x2_last = None
            last_y_mm = [None, None]
            for s in range(4):
                buf = s % 2
                wo_dep[buf] = wload(
                    "gpsimd", woq[:, ts(buf, 8192)],
                    wo[:, ts(s, 512)], 512,
                    waits=[wo_dep[buf], last_y_mm[buf]] + fence)
                for t in range(NT):
                    bk = pb[t % 2]
                    p.waits("tensor", [wo_dep[buf], a_done, bank_free[t % 2]])
                    for k in range(NK):
                        def f(E, k=k, t=t, bk=bk, buf=buf):
                            return E.matmul(
                                bk[:, 0:512],
                                arena[:, k * T + t * 128: k * T + t * 128 + 128],
                                woq[:, buf * 8192 + k * 512: buf * 8192 + (k + 1) * 512],
                                start=(k == 0), stop=(k == NK - 1))
                        dmm = p.op("tensor", f, inc=(k == NK - 1))
                    last_y_mm[buf] = dmm
                    xb = t % 2
                    dx = p.dma("sync", yx[:, ts(xb, 512)], x[ts(t, 128), ts(s, 512)],
                               waits=[yx_dep[xb]] + fence)
                    cn = 2 * t
                    d0 = p.op("vector", lambda E, xb=xb, cn=cn: E.tensor_scalar(
                        yst[:, ts(xb, 512)], yx[:, ts(xb, 512)],
                        stats[:, cn:cn + 1], stats[:, cn + 1:cn + 2],
                        ALU.add, ALU.mult),
                        waits=[dx, yst_dep[xb]] + fence)
                    yx_dep[xb] = d0
                    d1 = p.op("vector", lambda E, xb=xb, bk=bk: E.tensor_add(
                        yst[:, ts(xb, 512)], yst[:, ts(xb, 512)], bk[:, 0:512]),
                        waits=[d0, dmm])
                    bank_free[t % 2] = d1
                    dst_ = p.dma("sync", x2d[ts(t, 128), ts(s, 512)],
                                 yst[:, ts(xb, 512)], waits=[d1])
                    yst_dep[xb] = dst_
                    x2_last = dst_

            tstg_dep = [None, None]
            dT_last = None
            for i in range(HID2 // 128):
                slot = i % 2
                dtr = p.dma("scalar", tstg[:, slot * 2048: slot * 2048 + 2048],
                            wdown_bf[0:D, ts(i, 128)],
                            waits=[tstg_dep[slot]] + d_dbf + fence, transpose=True)
                dst_ = p.dma("sync", wdownT[ts(i, 128), 0:D],
                             tstg[:, slot * 2048: slot * 2048 + 2048], waits=[dtr])
                tstg_dep[slot] = dst_
                dT_last = dst_

        tr_h2 = ln_block(x2d, arena, ("plain-affine", n2w, n2b), 32)

        with ExitStack() as f1s:
            fe = f1s.enter_context
            a2 = fe(nc.sbuf_tensor("a2", [128, 43 * 1024], BF16))
            w12cb = fe(nc.sbuf_tensor("w12cb", [128, 4 * 2048], BF16))
            silu_t = fe(nc.sbuf_tensor("silu_t", [128, 2 * 512], F32))
            fence = p.fence_all()

            w12_dep = [None, None, None, None]
            silu_dep = [None, None]
            a2_last = None
            for ch in range(43):
                ncc = 128 if ch < 42 else 85
                buf = ch % 2
                dw1 = wload("gpsimd", w12cb[:, buf * 2048: buf * 2048 + NK * ncc],
                            w1[:, ch * 128: ch * 128 + ncc], ncc,
                            waits=[w12_dep[buf]] + fence)
                dw2 = wload("gpsimd",
                            w12cb[:, (2 + buf) * 2048: (2 + buf) * 2048 + NK * ncc],
                            w2[:, ch * 128: ch * 128 + ncc], ncc,
                            waits=[w12_dep[2 + buf]] + fence)
                ch_dep = None
                for half in range(2):
                    b1 = pb[half]
                    b2_ = pb[2 + half]
                    p.waits("tensor", [dw1, bank_free[half]] + tr_h2)
                    for k in range(NK):
                        def f(E, k=k, half=half, b1=b1, buf=buf, ncc=ncc):
                            return E.matmul(
                                b1[0:ncc, 0:512],
                                w12cb[:, buf * 2048 + k * ncc: buf * 2048 + (k + 1) * ncc],
                                arena[:, k * T + half * 512: k * T + (half + 1) * 512],
                                start=(k == 0), stop=(k == NK - 1))
                        du1 = p.op("tensor", f, inc=(k == NK - 1))
                    p.waits("tensor", [dw2, bank_free[2 + half]])
                    for k in range(NK):
                        def f(E, k=k, half=half, b2_=b2_, buf=buf, ncc=ncc):
                            return E.matmul(
                                b2_[0:ncc, 0:512],
                                w12cb[:, (2 + buf) * 2048 + k * ncc: (2 + buf) * 2048 + (k + 1) * ncc],
                                arena[:, k * T + half * 512: k * T + (half + 1) * 512],
                                start=(k == 0), stop=(k == NK - 1))
                        du2 = p.op("tensor", f, inc=(k == NK - 1))
                    dsl = p.op("scalar", lambda E, half=half, b1=b1, ncc=ncc:
                               E.activation(silu_t[0:ncc, ts(half, 512)],
                                            b1[0:ncc, 0:512], AF.Silu),
                               waits=[du1, silu_dep[half]] + fence)
                    bank_free[half] = dsl
                    da2 = p.op("vector",
                               lambda E, ch=ch, half=half, b2_=b2_, ncc=ncc:
                               E.tensor_mul(
                                   a2[0:ncc, ch * 1024 + half * 512: ch * 1024 + (half + 1) * 512],
                                   silu_t[0:ncc, ts(half, 512)],
                                   b2_[0:ncc, 0:512]),
                               waits=[dsl, du2] + fence)
                    bank_free[2 + half] = da2
                    silu_dep[half] = da2
                    ch_dep = da2
                w12_dep[buf] = ch_dep
                w12_dep[2 + buf] = ch_dep
                a2_last = ch_dep

            with ExitStack() as f1b:
                fb = f1b.enter_context
                w3b = fb(nc.sbuf_tensor("w3b", [128, 2 * 11008], BF16))
                fxt = fb(nc.sbuf_tensor("fxt", [128, 4 * 256], F32))
                fst = fb(nc.sbuf_tensor("fst", [128, 2 * 256], F32))
                fence = p.fence_all()
                w3_dep = [None, None]
                fx_dep = [None, None, None, None]
                fst_dep = [None, None]
                xB_last = None
                last_f_mm = [None, None]
                for s8 in range(8):
                    buf = s8 % 2
                    off = buf * 11008
                    dw3a = wload("gpsimd", w3b[:, off: off + 42 * 256],
                                 w3[0:5376, ts(s8, 256)], 256,
                                 waits=[w3_dep[buf], last_f_mm[buf]] + fence)
                    dw3b = p.dma("gpsimd", w3b[0:85, off + 42 * 256: off + 43 * 256],
                                 w3[5376:5461, ts(s8, 256)])
                    for t in range(NT):
                        bk = pb[4 + t % 2]
                        p.waits("tensor", [dw3a, dw3b, bank_free[4 + t % 2], a2_last])
                        for ch in range(43):
                            ncc = 128 if ch < 42 else 85

                            def f(E, ch=ch, t=t, bk=bk, off=off, ncc=ncc):
                                return E.matmul(
                                    bk[:, 0:256],
                                    a2[0:ncc, ch * 1024 + t * 128: ch * 1024 + t * 128 + 128],
                                    w3b[0:ncc, off + ch * 256: off + (ch + 1) * 256],
                                    start=(ch == 0), stop=(ch == 42))
                            dmm = p.op("tensor", f, inc=(ch == 42))
                        last_f_mm[buf] = dmm
                        xb = t % 2
                        dx2 = p.dma("sync", fxt[:, ts(xb, 256)],
                                    x2d[ts(t, 128), ts(s8, 256)],
                                    waits=[fx_dep[xb]] + fence)
                        dxr = p.dma("sync", fxt[:, ts(2 + xb, 256)],
                                    x[ts(t, 128), ts(s8, 256)],
                                    waits=[fx_dep[2 + xb]] + fence)
                        d0 = p.op("vector", lambda E, xb=xb, bk=bk: E.tensor_add(
                            fst[:, ts(xb, 256)], fxt[:, ts(xb, 256)], bk[:, 0:256]),
                            waits=[dx2, dmm, fst_dep[xb]] + fence)
                        bank_free[4 + t % 2] = d0
                        fx_dep[xb] = d0
                        d1 = p.op("vector", lambda E, xb=xb: E.tensor_add(
                            fst[:, ts(xb, 256)], fst[:, ts(xb, 256)],
                            fxt[:, ts(2 + xb, 256)]), waits=[dxr, d0])
                        fx_dep[2 + xb] = d1
                        dst_ = p.dma("sync", xBd[ts(t, 128), ts(s8, 256)],
                                     fst[:, ts(xb, 256)], waits=[d1])
                        fst_dep[xb] = dst_
                        xB_last = dst_
                    w3_dep[buf] = xB_last

        tr_xb = ln_block(xBd, arena, None, 32)

        with ExitStack() as f2s:
            fe = f2s.enter_context
            a3 = fe(nc.sbuf_tensor("a3", [128, 64 * 512], BF16))
            wwin = fe(nc.sbuf_tensor("wwin", [128, 2 * 16384], BF16,
                                     is_transpose_dest=True))
            f2x = fe(nc.sbuf_tensor("f2x", [128, 2 * 512], F32))
            f2st = fe(nc.sbuf_tensor("f2st", [128, 2 * 512], F32))
            silu2 = fe(nc.sbuf_tensor("silu2", [128, 512], F32))
            fence = p.fence_all()

            win_dep = [None, None]
            f2x_dep = [None, None]
            f2st_dep = [None, None]
            a3_busy = [None] * 64
            wwin_free = None
            silu2_dep = None
            out_last = None
            for half in range(2):
                for w in range(16):
                    buf = w % 2
                    offg = buf * 16384
                    offu = buf * 16384 + 8192
                    gparts = []
                    uparts = []
                    for k in range(NK):
                        dgp = p.dma("scalar",
                                    wwin[:, offg + k * 512: offg + (k + 1) * 512],
                                    wgate_bf[ts(w, 512), ts(k, 128)],
                                    waits=[win_dep[buf], wwin_free] + d_gbf + fence,
                                    transpose=True)
                        gparts.append(dgp)
                        dup = p.dma("scalar",
                                    wwin[:, offu + k * 512: offu + (k + 1) * 512],
                                    wup_bf[ts(w, 512), ts(k, 128)],
                                    waits=d_ubf, transpose=True)
                        uparts.append(dup)
                    last = None
                    for hh in range(4):
                        hid = w * 4 + hh
                        bg = pb[hh % 2]
                        bu = pb[2 + hh % 2]
                        p.waits("tensor",
                                gparts + [bank_free[hh % 2], a3_busy[hid]] + tr_xb)
                        for k in range(NK):
                            def f(E, k=k, half=half, bg=bg, offg=offg, hh=hh):
                                return E.matmul(
                                    bg[:, 0:512],
                                    wwin[:, offg + k * 512 + hh * 128: offg + k * 512 + (hh + 1) * 128],
                                    arena[:, k * T + half * 512: k * T + (half + 1) * 512],
                                    start=(k == 0), stop=(k == NK - 1))
                            dg_ = p.op("tensor", f, inc=(k == NK - 1))
                        p.waits("tensor", uparts + [bank_free[2 + hh % 2]])
                        for k in range(NK):
                            def f(E, k=k, half=half, bu=bu, offu=offu, hh=hh):
                                return E.matmul(
                                    bu[:, 0:512],
                                    wwin[:, offu + k * 512 + hh * 128: offu + k * 512 + (hh + 1) * 128],
                                    arena[:, k * T + half * 512: k * T + (half + 1) * 512],
                                    start=(k == 0), stop=(k == NK - 1))
                            du_ = p.op("tensor", f, inc=(k == NK - 1))
                        wwin_last_mm = du_
                        dsl = p.op("scalar", lambda E, bg=bg: E.activation(
                            silu2[:, 0:512], bg[:, 0:512], AF.Silu),
                            waits=[dg_, silu2_dep] + fence)
                        bank_free[hh % 2] = dsl
                        da3 = p.op("vector", lambda E, hid=hid, bu=bu: E.tensor_mul(
                            a3[:, ts(hid, 512)], silu2[:, 0:512], bu[:, 0:512]),
                            waits=[dsl, du_] + fence)
                        silu2_dep = da3
                        bank_free[2 + hh % 2] = da3
                        a3_busy[hid] = da3
                        last = da3
                    win_dep[buf] = last
                    wwin_free = wwin_last_mm
                for s in range(4):
                    dwd0 = wload("gpsimd", wwin[:, 0:16384],
                                 wdownT[0:4096, ts(s, 512)], 512,
                                 waits=[win_dep[0], win_dep[1], dT_last, wwin_free])
                    dwd1 = wload("gpsimd", wwin[:, 16384:32768],
                                 wdownT[4096:8192, ts(s, 512)], 512)
                    win_dep[0] = None
                    win_dep[1] = None
                    for tt in range(4):
                        t = half * 4 + tt
                        bk = pb[4 + tt % 2]
                        p.waits("tensor",
                                [dwd0, dwd1, bank_free[4 + tt % 2], a3_busy[63]])
                        for hid in range(64):
                            def f(E, hid=hid, tt=tt, bk=bk):
                                return E.matmul(
                                    bk[:, 0:512],
                                    a3[:, hid * 512 + tt * 128: hid * 512 + (tt + 1) * 128],
                                    wwin[:, hid * 512: (hid + 1) * 512],
                                    start=(hid == 0), stop=(hid == 63))
                            dmm = p.op("tensor", f, inc=(hid == 63))
                        wwin_free = dmm
                        xb = tt % 2
                        dxB = p.dma("sync", f2x[:, ts(xb, 512)],
                                    xBd[ts(t, 128), ts(s, 512)],
                                    waits=[f2x_dep[xb]] + fence)
                        d0 = p.op("vector", lambda E, xb=xb, bk=bk: E.tensor_add(
                            f2st[:, ts(xb, 512)], f2x[:, ts(xb, 512)],
                            bk[:, 0:512]),
                            waits=[dxB, dmm, f2st_dep[xb]] + fence)
                        bank_free[4 + tt % 2] = d0
                        f2x_dep[xb] = d0
                        dst_ = p.dma("sync", out[ts(t, 128), ts(s, 512)],
                                     f2st[:, ts(xb, 512)], waits=[d0])
                        f2st_dep[xb] = dst_
                        out_last = dst_
                for i in range(64):
                    a3_busy[i] = out_last

            p.waits("sync", [out_last] +
                    [(f"ss{l}", p.cnt[f"ss{l}"]) for l in range(DMA_LANES)])

        with nc.Block() as blk:
            @blk.sync
            def _(E):
                for f in p.ops["sync"]:
                    f(E)

            @blk.scalar
            def _(E):
                for f in p.ops["scalar"]:
                    f(E)

            @blk.vector
            def _(E):
                for f in p.ops["vector"]:
                    f(E)

            @blk.tensor
            def _(E):
                for f in p.ops["tensor"]:
                    f(E)

            @blk.gpsimd
            def _(E):
                for f in p.ops["gpsimd"]:
                    f(E)

    return nc


def _host_tables():
    gam = 1.0 - 2.0 ** (-5.0 - np.arange(H, dtype=np.float64))
    f_ = np.arange(512)
    p_ = np.arange(128)
    decs = np.zeros((H, 8, 128, 512), np.float32)
    for h in range(H):
        for r in range(8):
            d_ = (r - 3) * 128 + f_[None, :] - p_[:, None]
            decs[h, r] = np.where(d_ >= 0, gam[h] ** np.maximum(d_, 0), 0.0
                                  ).astype(np.float32)
    a_ = np.arange(T, dtype=np.float64)
    gq = np.concatenate([(g ** (a_ + 1.0)) for g in gam]).astype(ml_dtypes.bfloat16)
    gq = np.ascontiguousarray(gq.reshape(1, H * T))
    return decs, gq, gam


def _kg0_table(gam):
    kg0 = np.zeros((128, NT * H), np.float32)
    p_ = np.arange(128, dtype=np.float64)
    for t in range(NT):
        pos = t * 128 + p_
        for h in range(H):
            kg0[:, t * H + h] = (gam[h] ** (1023.0 - pos)) * SCL
    return kg0


def _coef_for(seq, loc, gam):
    cf = np.zeros((128, 8 * H), np.float32)
    for r in range(8):
        if r // 4 != seq or r % 4 >= loc:
            continue
        c = r % 4
        for h in range(H):
            cf[:, r * H + h] = gam[h] ** (1024.0 * (loc - 1 - c))
    return cf


def kernel(L0_attn_x, L0_attn_norm1_w, L0_attn_norm1_b, L0_attn_wq, L0_attn_wk,
           L0_attn_wv, L0_attn_wg, L0_attn_wo, L0_attn_norm2_w, L0_attn_norm2_b,
           L0_attn_w1, L0_attn_w2, L0_attn_w3, L0_ffn_w_gate, L0_ffn_w_up,
           L0_ffn_w_down):
    xfull = np.ascontiguousarray(np.asarray(L0_attn_x, np.float32))
    decs, gq, gam = _host_tables()
    f32 = lambda a: np.ascontiguousarray(np.asarray(a, np.float32))
    shared = dict(
        wq=f32(L0_attn_wq), wk=f32(L0_attn_wk), wv=f32(L0_attn_wv),
        wg=f32(L0_attn_wg), wo=f32(L0_attn_wo),
        w1=f32(L0_attn_w1), w2=f32(L0_attn_w2), w3=f32(L0_attn_w3),
        wgate=f32(L0_ffn_w_gate), wup=f32(L0_ffn_w_up), wdown=f32(L0_ffn_w_down),
        dec=decs, gqrow=gq,
        n1w=f32(L0_attn_norm1_w).reshape(1, D),
        n1b=f32(L0_attn_norm1_b).reshape(1, D),
        n2w=f32(L0_attn_norm2_w).reshape(1, D),
        n2b=f32(L0_attn_norm2_b).reshape(1, D),
    )
    kg0t = _kg0_table(gam)
    in_maps = []
    for c in range(8):
        seq, loc = c // 4, c % 4
        m = dict(shared)
        m["x"] = np.ascontiguousarray(xfull[seq, loc * T:(loc + 1) * T, :])
        m["kg0"] = kg0t
        m["coef"] = _coef_for(seq, loc, gam)
        in_maps.append(m)

    nc = build_program()
    res = run_bass_kernel_spmd(nc, in_maps, core_ids=list(range(8)))
    outs = [np.asarray(res.results[c]["out"], np.float32) for c in range(8)]
    return np.stack([np.concatenate(outs[0:4], 0), np.concatenate(outs[4:8], 0)], 0)


if __name__ == "__main__":
    nc = build_program()
    n = sum(1 for _ in nc.all_instructions())
    print("instructions:", n)
